# version 25
# baseline (speedup 1.0000x reference)
import contextlib
import numpy as np
import concourse.bass as bass
import concourse.mybir as mybir
from concourse.bass_utils import run_bass_kernel_spmd

F32 = mybir.dt.float32
BF16 = mybir.dt.bfloat16
I32 = mybir.dt.int32
AF = mybir.ActivationFunctionType
ALU = mybir.AluOpType
AX = mybir.AxisListType

P = 128
DM = 1024
INNER = 2048
HID = 3072
DK = DM // P
IK = INNER // P
HK = HID // P
L = 512
FFS = 1.0 / 127.0
EPS_LN = 1e-5
EPS_RMS = 1e-6
NG = 8
NQ = 8

SEM_NAMES = (["T", "V", "S", "Gc"] + [f"G{i}" for i in range(NG)]
             + [f"Q{i}" for i in range(NQ)])


class Em:

    def __init__(self):
        self.streams = {e: [] for e in "TVSGQ"}
        self.cnt = {s: 0 for s in SEM_NAMES}
        self.know = {e: {} for e in "TVSGQ"}
        self.clock = {}
        self.g_rr = 0
        self.q_rr = 0
        self.self_known = {"V": 0, "S": 0}

    def emit(self, stream, fn, waits=(), inc=None, amt=1, forced=(), is_dma=False):
        k = self.know[stream]
        pruned = []

        def merge(sem, val):
            for s2, v2 in self.clock.get((sem, val), {sem: val}).items():
                if k.get(s2, 0) < v2:
                    k[s2] = v2

        if stream in ("V", "S") and fn is not None and not is_dma:
            prev = self.cnt[stream]
            if prev:
                pruned.append((stream, prev))
                merge(stream, prev)
                self.self_known[stream] = prev
            inc, amt = stream, 1
        for w in forced:
            if not w:
                continue
            pruned.append(w)
            merge(*w)
        for w in waits:
            if not w:
                continue
            sem, val = w
            if is_dma and sem == stream and stream in ("V", "S"):
                if val <= self.self_known[stream]:
                    continue
                pruned.append((sem, val))
                self.self_known[stream] = val
                merge(sem, val)
                continue
            if k.get(sem, 0) >= val:
                continue
            pruned.append((sem, val))
            merge(sem, val)
        tok = None
        if inc:
            self.cnt[inc] += amt
            tok = (inc, self.cnt[inc])
            snap = dict(k)
            snap[inc] = self.cnt[inc]
            self.clock[tok] = snap
            if inc == stream and stream in ("T", "V", "S"):
                k[inc] = self.cnt[inc]
        self.streams[stream].append((fn, pruned, tok, amt))
        return tok

    def alloc_sem(self, fam):
        if fam == "G":
            sem = f"G{self.g_rr % NG}"
            self.g_rr += 1
        else:
            sem = f"Q{self.q_rr % NQ}"
            self.q_rr += 1
        return sem

    def dma(self, stream, fn, waits=(), sem=None):
        if sem is None:
            sem = self.alloc_sem("G" if stream == "G" else "Q")
        prev = [(sem, self.cnt[sem])] if self.cnt[sem] else []
        return self.emit(stream, fn, waits, inc=sem, amt=16, forced=prev, is_dma=True)

    def replay(self, stream, eng, semh):
        for fn, waits, tok, amt in self.streams[stream]:
            for sem, val in waits:
                eng.wait_ge(semh[sem], val)
            inst = fn(eng) if fn is not None else None
            if tok is not None:
                inst.then_inc(semh[tok[0]], amt)


class Bank:
    __slots__ = ("w", "r")

    def __init__(self):
        self.w = None
        self.r = None


N_UNITS = {"wug_in": DK, "wug_g": DK, "wb": IK, "wc": IK, "wd": IK,
           "wout": DK, "wfg": 12, "wfu": 12, "wfd": DK}
UNIT_SZ = {r: 2048 for r in N_UNITS}
UNIT_SZ["wfd"] = HK * P
W_DT = {"wfg": I32, "wfu": I32, "wfd": I32}
SLOTS = {"wug_in": [0, 2048],
         "wout": [0, 2048, 4096, 6144],
         "wug_g": [4096, 6144],
         "wfg": [4096, 6144],
         "wb": [8192, 10240],
         "wc": [12288, 14336],
         "wfu": [20480, 22528],
         "wd": [16384, 18432],
         "wfd": [16384, 8192]}


def build_nc(n_chunks=8):
    toks = n_chunks * L
    nc = bass.Bass("TRN2", target_bir_lowering=False)
    d_x = nc.dram_tensor("x", [DM, toks], F32, kind="ExternalInput").ap()
    d_y = nc.dram_tensor("y", [DM, toks], F32, kind="ExternalOutput").ap()
    d_bias = nc.dram_tensor("bias", [P, 96], F32, kind="ExternalInput").ap()
    d_nw = nc.dram_tensor("nw", [P, DK], F32, kind="ExternalInput").ap()
    d_ob = nc.dram_tensor("ob", [P, DK], F32, kind="ExternalInput").ap()

    cw = {r: nc.dram_tensor(f"cw_{r}", [n, P, UNIT_SZ[r]], BF16, kind="Internal").ap()
          for r, n in N_UNITS.items()}
    dw = {r: nc.dram_tensor(f"dw_{r}", [n, P, UNIT_SZ[r]], W_DT.get(r, F32),
                            kind="ExternalInput").ap()
          for r, n in N_UNITS.items()}
    FAM_Q = {"wb": "Q", "wc": "Q", "wd": "Q", "wug_in": "Q", "wug_g": "Q",
             "wout": "G", "wfg": "G", "wfu": "G", "wfd": "G"}

    with contextlib.ExitStack() as ctx:
        e = ctx.enter_context
        s_bias = e(nc.sbuf_tensor("s_bias", [P, 96], F32))
        s_a = e(nc.sbuf_tensor("s_a", [P, IK], F32))
        s_h = e(nc.sbuf_tensor("s_h", [P, IK], F32))
        s_nw = e(nc.sbuf_tensor("s_nw", [P, DK], F32))
        s_ob = e(nc.sbuf_tensor("s_ob", [P, DK], F32))
        s_ones = e(nc.sbuf_tensor("s_ones", [P, P], BF16))
        s_x = e(nc.sbuf_tensor("s_x", [P, DK, L], F32))
        s_xbf = e(nc.sbuf_tensor("s_xbf", [P, DK, L], BF16))
        s_w = e(nc.sbuf_tensor("s_w", [P, DK, L], F32))
        s_x1 = e(nc.sbuf_tensor("s_x1", [P, DK, L], F32))
        s_x2 = e(nc.sbuf_tensor("s_x2", [P, DK, L], F32))
        s_xn = e(nc.sbuf_tensor("s_xn", [P, DK, L], BF16))
        s_xn2 = e(nc.sbuf_tensor("s_xn2", [P, DK, L], BF16))
        s_stt = e(nc.sbuf_tensor("s_stt", [P, 7, L], F32))
        s_sr = e(nc.sbuf_tensor("s_sr", [P, 2, L], F32))
        s_uy = e(nc.sbuf_tensor("s_uy", [P, 2 * IK * L], BF16))
        s_bsb = e(nc.sbuf_tensor("s_bsb", [P, 2, L], F32))
        s_hs = e(nc.sbuf_tensor("s_hs", [P, 2, L], F32))
        s_csb = e(nc.sbuf_tensor("s_csb", [P, 2, L], F32))
        s_yv = e(nc.sbuf_tensor("s_yv", [P, 2, L], F32))
        s_gsb = e(nc.sbuf_tensor("s_gsb", [P, 2, L], F32))
        s_wa = e(nc.sbuf_tensor("s_wa", [P, 24576], BF16))
        pbank = [e(nc.psum_tensor(f"pb{i}", [P, 512], F32)) for i in range(8)]
        semh = {s: e(nc.semaphore(s)) for s in SEM_NAMES}

        u_v = s_uy[:, 0:IK * L].rearrange("p (k t) -> p k t", k=IK)
        y_v = s_uy[:, IK * L:2 * IK * L].rearrange("p (k t) -> p k t", k=IK)
        hf_v = s_uy[:, 0:HK * L].rearrange("p (k t) -> p k t", k=HK)
        ua0 = s_uy[:, 0:DK * L].rearrange("p (k t) -> p k t", k=DK)
        ua1 = s_uy[:, DK * L:2 * DK * L].rearrange("p (k t) -> p k t", k=DK)
        M1, R1, RMS1, M2, R2, T0c, Vc = range(7)

        em = Em()
        banks = [Bank() for _ in range(8)]

        occ = {}
        pending_wb = {}

        def slot_of(role, seq):
            sl = SLOTS[role]
            return (sl[seq % len(sl)], UNIT_SZ[role])

        def slot_flat(slot):
            return s_wa[:, slot[0]:slot[0] + slot[1]]

        def wview(role, seq, kcnt):
            return slot_flat(slot_of(role, seq)).rearrange("p (k j) -> p k j", k=kcnt)

        def slot_used(role, seq, tok):
            occ.setdefault(slot_of(role, seq), {}).setdefault(role, []).append(tok)

        def overlaps(a, b):
            return a[0] < b[0] + b[1] and b[0] < a[0] + a[1]

        def load_w(role, seq, unit, ch, sem=None):
            slot = slot_of(role, seq)
            ws = []
            for oslot in list(occ):
                if not overlaps(oslot, slot):
                    continue
                o = occ[oslot]
                for r2 in list(o):
                    key2 = (r2, oslot)
                    if key2 in pending_wb:
                        wunit, ltok = pending_wb.pop(key2)
                        wtok = em.dma("Q", (lambda r2=r2, wunit=wunit, sf=slot_flat(oslot):
                                            lambda g: g.dma_start(cw[r2][wunit], sf))(),
                                      waits=[ltok])
                        o.setdefault(r2, []).append(wtok)
                for lst in o.values():
                    ws.extend(lst)
                del occ[oslot]
            occ[slot] = {role: []}
            if ch == 0:
                tok = em.dma("G", (lambda dst=slot_flat(slot), role=role, unit=unit:
                                   lambda g: g.dma_start(dst, dw[role][unit]))(),
                             waits=ws, sem=sem)
                pending_wb[(role, slot)] = (unit, tok)
            else:
                tok = em.dma(FAM_Q[role], (lambda dst=slot_flat(slot), role=role,
                                           unit=unit: lambda g: g.dma_start(dst, cw[role][unit]))(),
                             waits=ws, sem=sem)
            return tok

        def flush_wb(roles):
            for key2 in list(pending_wb):
                r2, slot = key2
                if r2 not in roles:
                    continue
                wunit, ltok = pending_wb.pop(key2)
                wtok = em.dma("Q", (lambda r2=r2, wunit=wunit, sf=slot_flat(slot):
                                    lambda g: g.dma_start(cw[r2][wunit], sf))(),
                              waits=[ltok])
                occ.setdefault(slot, {}).setdefault(r2, []).append(wtok)

        tok_bias = em.dma("Q", lambda g: g.dma_start(s_bias[:], d_bias))
        tok_nw = em.dma("Q", lambda g: g.dma_start(s_nw[:], d_nw))
        tok_ob = em.dma("Q", lambda g: g.dma_start(s_ob[:], d_ob))
        tok_ones = em.emit("V", lambda v: v.memset(s_ones[:], 1.0), inc="V")
        tok_a = em.emit("S", lambda s: s.activation(
            s_a[:], s_bias[:, 80:96], AF.Sigmoid, scale=-1.0),
            waits=[tok_bias], inc="S")
        em.emit("V", lambda v: v.memset(s_h[:], 0.0))

        x_war = [() for _ in range(DK)]
        xbf_war = [None]
        xn_war = [None]
        xn2_war = [None]
        x2_war = [None] * DK
        scr_war = [None]
        r_war = [None, None]

        def mm(bank_i, lhsT, rhs, start, stop, waits=(), inc=False):
            b = banks[bank_i]
            ws = list(waits)
            if start:
                ws.append(b.r)
            tok = em.emit(
                "T",
                (lambda bi=bank_i, lh=lhsT, rh=rhs, st=start, sp=stop:
                 lambda t: t.matmul(pbank[bi][:], lh, rh, start=st, stop=sp))(),
                waits=ws, inc="T" if inc else None)
            if stop and inc:
                b.w = tok
            return tok

        def ln_chain(b_sum, b_sq, mc, rc_col, rms_col, ln):
            m_ = s_stt[:, mc]
            t0 = s_stt[:, T0c]
            v_ = s_stt[:, Vc]
            tok_m = em.emit("V", (lambda m_=m_, b=b_sum: lambda v: v.tensor_scalar(
                m_, pbank[b][:], 1.0 / DM, None, ALU.mult))(),
                waits=[banks[b_sum].w, scr_war[0]], inc="V")
            banks[b_sum].r = tok_m
            tok_t0 = em.emit("V", (lambda t0=t0, b=b_sq: lambda v: v.tensor_scalar(
                t0, pbank[b][:], 1.0 / DM, None, ALU.mult))(),
                waits=[banks[b_sq].w], inc="V")
            banks[b_sq].r = tok_t0
            em.emit("V", (lambda v_=v_, m_=m_: lambda v: v.scalar_tensor_tensor(
                v_, m_, -1.0, m_, ALU.mult, ALU.mult))())
            tok_vp = em.emit("V", (lambda v_=v_, t0=t0: lambda v: v.scalar_tensor_tensor(
                v_, t0, EPS_LN, v_, ALU.add, ALU.add))(), inc="V")
            em.emit("S", (lambda t0=t0, v_=v_: lambda s: s.activation(
                t0, v_, AF.Ln))(), waits=[tok_vp])
            tok_r = em.emit("S", (lambda rc_col=rc_col, t0=t0: lambda s: s.activation(
                s_stt[:, rc_col], t0, AF.Exp, scale=-0.5))(),
                waits=[r_war[ln]], inc="S")
            tok_rms = None
            if rms_col is not None:
                tok_rc = em.emit("S", (lambda v_=v_, t0=t0: lambda s: s.activation(
                    v_, t0, AF.Exp, scale=-1.0))(), inc="S")
                tok_q = em.emit("V", (lambda v_=v_: lambda v: v.tensor_scalar(
                    v_, v_, -EPS_LN, 1.0 + EPS_RMS, ALU.mult, ALU.add))(),
                    waits=[tok_rc], inc="V")
                em.emit("S", (lambda t0=t0, v_=v_: lambda s: s.activation(
                    t0, v_, AF.Ln))(), waits=[tok_q])
                tok_rms = em.emit("S", (lambda rms_col=rms_col, t0=t0: lambda s: s.activation(
                    s_stt[:, rms_col], t0, AF.Exp, scale=-0.5))(), inc="S")
            scr_war[0] = tok_rms if tok_rms is not None else tok_r
            return tok_m, tok_r, tok_rms

        def emit_x_copies(c):
            t0_ = c * L
            st = {"c": c, "tok_x": [], "tok_cp": [], "tok_xn": [],
                  "tok_summ": [], "tok_sq": [None] * DK}
            for f in range(DK):
                st["tok_x"].append(em.dma("Q", (lambda f=f, t0_=t0_: lambda g: g.dma_start(
                    s_x[:, f], d_x[P * f:P * (f + 1), t0_:t0_ + L]))(),
                    waits=list(x_war[f])))
            for f in range(DK):
                st["tok_cp"].append(em.emit("V", (lambda f=f: lambda v: v.tensor_copy(
                    s_xbf[:, f], s_x[:, f]))(),
                    waits=[st["tok_x"][f], xbf_war[0]], inc="V"))
            return st

        def emit_stats1_mms(st):
            for f in range(DK):
                st["tok_summ"].append(mm(6, s_ones[:], s_xbf[:, f], f == 0, f == DK - 1,
                                         waits=[st["tok_cp"][f], tok_ones], inc=True))

        def emit_sq1(st, f):
            st["tok_sq"][f] = em.emit("V", (lambda f=f: lambda v: v.tensor_mul(
                s_xbf[:, f], s_xbf[:, f], s_xbf[:, f]))(),
                waits=[st["tok_summ"][f]], inc="V")

        def emit_stats1_tail(st):
            tok_sqm = None
            for f in range(DK):
                tok_sqm = mm(7, s_ones[:], s_xbf[:, f], f == 0, f == DK - 1,
                             waits=[st["tok_sq"][f]], inc=True)
            xbf_war[0] = tok_sqm
            st["ln"] = ln_chain(6, 7, M1, R1, RMS1, 0)

        def fb_ops(st, f):
            tok_m, tok_r, tok_rms = st["ln"]
            sl = f % 2
            em.emit("V", (lambda f=f, sl=sl: lambda v: v.tensor_sub(
                s_sr[:, sl], s_x[:, f], s_stt[:, M1]))(), waits=[tok_m])
            em.emit("V", (lambda sl=sl: lambda v: v.tensor_mul(
                s_sr[:, sl], s_sr[:, sl], s_stt[:, R1]))(), waits=[tok_r])
            tok = em.emit("V", (lambda f=f, sl=sl: lambda v: v.scalar_tensor_tensor(
                s_xn[:, f], s_sr[:, sl], s_nw[:, f:f + 1], s_stt[:, RMS1],
                ALU.mult, ALU.mult))(),
                waits=[tok_rms, tok_nw, xn_war[0]], inc="V")
            st["tok_xn"].append(tok)
            if f == DK - 1:
                r_war[0] = tok

        def fb_w(st, f):
            tok_w = em.emit("V", (lambda f=f, sl=f % 2: lambda v: v.tensor_add(
                s_w[:, f], s_x[:, f], s_sr[:, sl]))(), inc="V")
            x_war[f] = (tok_w,)

        FB_SCHED = [["f0", "f1", "w0"], ["f2", "w1", "f3", "w2"],
                    ["f4", "w3", "f5", "w4"], ["f6", "w5", "f7", "w6", "w7"]]

        st_cur = emit_x_copies(0)
        st_cur["wug_i"], st_cur["wug_g"] = {}, {}
        for gg in range(2):
            sp = em.alloc_sem("G")
            st_cur["wug_i"][gg] = load_w("wug_in", gg, gg, 0, sem=sp)
            st_cur["wug_g"][gg] = load_w("wug_g", gg, gg, 0, sem=sp)
        emit_stats1_mms(st_cur)
        for f in range(DK):
            emit_sq1(st_cur, f)
        emit_stats1_tail(st_cur)
        for f in range(DK):
            fb_ops(st_cur, f)
            fb_w(st_cur, f)
        st_cur["wb"], st_cur["wc"], st_cur["wd"] = {}, {}, {}
        for j in range(2):
            sp = em.alloc_sem("G")
            st_cur["wb"][j] = load_w("wb", j, j, 0, sem=sp)
            st_cur["wc"][j] = load_w("wc", j, j, 0, sem=sp)
            st_cur["wd"][j] = load_w("wd", j, j, 0, sem=sp)

        wug_in = s_wa[:, 0:4096].rearrange("p (b k j) -> p b k j", b=2, k=DK)
        wug_g = s_wa[:, 4096:8192].rearrange("p (b k j) -> p b k j", b=2, k=DK)
        wfg_v = s_wa[:, 4096:8192].rearrange("p (b k j) -> p b k j", b=2, k=DK)
        wfu_v = s_wa[:, 20480:24576].rearrange("p (b k j) -> p b k j", b=2, k=DK)
        for c in range(n_chunks):
            t0_ = c * L

            tok_wug_i, tok_wug_g = st_cur["wug_i"], st_cur["wug_g"]
            tok_xn = st_cur["tok_xn"]
            tok_u = [None] * IK
            gsb_war = [None, None]
            for jg in range(IK):
                gg, off = jg // 2, 128 * (jg % 2)
                bu, bg = jg % 2, 2 + jg % 2
                for k in range(DK):
                    mm(bu, wug_in[:, gg % 2, k, off:off + 128], s_xn[:, k, :],
                       k == 0, k == DK - 1,
                       waits=([tok_wug_g[gg], tok_xn[k]] if jg == 0 else
                              ([tok_wug_g[gg]] if k == 0 else [])),
                       inc=(k == DK - 1))
                slot_used("wug_in", gg, banks[bu].w)
                for k in range(DK):
                    mm(bg, wug_g[:, gg % 2, k, off:off + 128], s_xn[:, k, :],
                       k == 0, k == DK - 1,
                       waits=[tok_wug_g[gg]] if k == 0 else [],
                       inc=(k == DK - 1))
                slot_used("wug_g", gg, banks[bg].w)
                if jg % 2 == 1 and gg + 2 < DK:
                    g2 = gg + 2
                    sp = em.alloc_sem("G" if c == 0 else "Q")
                    tok_wug_i[g2] = load_w("wug_in", g2, g2, c, sem=sp)
                    tok_wug_g[g2] = load_w("wug_g", g2, g2, c, sem=sp)
                tok_sig = em.emit("S", (lambda jg=jg, bg=bg: lambda s: s.activation(
                    s_gsb[:, jg % 2], pbank[bg][:], AF.Sigmoid,
                    bias=s_bias[:, 16 + jg:17 + jg]))(),
                    waits=[banks[bg].w, gsb_war[jg % 2]], inc="S")
                tok_u[jg] = em.emit("V", (lambda jg=jg, bu=bu: lambda v: v.scalar_tensor_tensor(
                    u_v[:, jg, :], pbank[bu][:], s_bias[:, jg:jg + 1], s_gsb[:, jg % 2],
                    ALU.add, ALU.mult))(),
                    waits=[banks[bu].w, tok_sig], inc="V")
                banks[bu].r = tok_u[jg]
                banks[bg].r = tok_u[jg]
                gsb_war[jg % 2] = tok_u[jg]

            st_next = emit_x_copies(c + 1) if c + 1 < n_chunks else None

            tok_wb, tok_wc, tok_wd = st_cur["wb"], st_cur["wc"], st_cur["wd"]
            tok_scan = [None] * IK
            tok_yv = [None] * IK
            tok_y = [None] * IK
            tok_wout = {}
            for j in range(IK):
                bb_, bc_, bd_ = j % 2, 2 + j % 2, 4 + j % 2
                for k in range(IK):
                    mm(bb_, wview("wb", j, IK)[:, k, :], u_v[:, k, :], k == 0, k == IK - 1,
                       waits=[tok_wd[j], tok_u[-1]] if k == 0 else [], inc=(k == IK - 1))
                slot_used("wb", j, banks[bb_].w)
                tok_bsb = em.emit("S", (lambda j=j, bb_=bb_: lambda s: s.activation(
                    s_bsb[:, j % 2], pbank[bb_][:], AF.Identity,
                    bias=s_bias[:, 32 + j:33 + j]))(),
                    waits=[banks[bb_].w, tok_scan[j - 2] if j >= 2 else None], inc="S")
                tok_scan[j] = em.emit("V", (lambda j=j: lambda v: v.tensor_tensor_scan(
                    s_hs[:, j % 2], s_a[:, j:j + 1].broadcast_to((P, L)),
                    s_bsb[:, j % 2], s_h[:, j:j + 1], ALU.mult, ALU.add))(),
                    waits=[tok_bsb, tok_a], inc="V")
                em.emit("V", (lambda j=j: lambda v: v.tensor_copy(
                    s_h[:, j:j + 1], s_hs[:, j % 2, L - 1:L]))())
                for k in range(IK):
                    mm(bc_, wview("wc", j, IK)[:, k, :], u_v[:, k, :], k == 0, k == IK - 1,
                       waits=[tok_wc[j]] if k == 0 else [], inc=(k == IK - 1))
                slot_used("wc", j, banks[bc_].w)
                tok_csb = em.emit("S", (lambda j=j, bc_=bc_: lambda s: s.activation(
                    s_csb[:, j % 2], pbank[bc_][:], AF.Identity,
                    bias=s_bias[:, 48 + j:49 + j]))(),
                    waits=[banks[bc_].w, tok_yv[j - 2] if j >= 2 else None], inc="S")
                for k in range(IK):
                    mm(bd_, wview("wd", j, IK)[:, k, :], u_v[:, k, :], k == 0, k == IK - 1,
                       waits=[tok_wd[j]] if k == 0 else [], inc=(k == IK - 1))
                slot_used("wd", j, banks[bd_].w)
                if j + 2 < IK:
                    j2 = j + 2
                    sp = em.alloc_sem("G" if c == 0 else "Q")
                    tok_wb[j2] = load_w("wb", j2, j2, c, sem=sp)
                    tok_wc[j2] = load_w("wc", j2, j2, c, sem=sp)
                    tok_wd[j2] = load_w("wd", j2, j2, c, sem=sp)
                elif j == IK - 2:
                    for sq_ in range(4):
                        tok_wout[sq_] = load_w("wout", sq_, sq_, c)
                tok_yv[j] = em.emit("V", (lambda j=j: lambda v: v.tensor_mul(
                    s_yv[:, j % 2], s_csb[:, j % 2], s_hs[:, j % 2]))(),
                    waits=[tok_csb], inc="V")
                tok_y[j] = em.emit("V", (lambda j=j, bd_=bd_: lambda v: v.scalar_tensor_tensor(
                    y_v[:, j, :], pbank[bd_][:], s_bias[:, 64 + j:65 + j], s_yv[:, j % 2],
                    ALU.add, ALU.add))(),
                    waits=[banks[bd_].w], inc="V")
                banks[bb_].r = tok_y[j]
                banks[bc_].r = tok_y[j]
                banks[bd_].r = tok_y[j]

            tok_x1 = [None] * DK
            tok_cp2 = [None] * DK
            tok_sq2 = [None] * DK
            for j in range(DK):
                bo = j % 2
                for k in range(IK):
                    ws = []
                    if k == 0:
                        ws = [tok_wout[j]]
                    if j == 0:
                        ws = ws + [tok_y[k]]
                    mm(bo, wview("wout", j, IK)[:, k, :], y_v[:, k, :],
                       k == 0, k == IK - 1, waits=ws, inc=(k == IK - 1))
                slot_used("wout", j, banks[bo].w)
                if j + 4 < DK:
                    tok_wout[j + 4] = load_w("wout", j + 4, j + 4, c)
                tok_x1[j] = em.emit("V", (lambda j=j, bo=bo: lambda v: v.scalar_tensor_tensor(
                    s_x1[:, j], pbank[bo][:], s_ob[:, j:j + 1], s_w[:, j],
                    ALU.add, ALU.add))(),
                    waits=[banks[bo].w, tok_ob], inc="V")
                tok_cp2[j] = em.emit("S", (lambda j=j: lambda s: s.activation(
                    ua0[:, j], s_x1[:, j], AF.Identity))(),
                    waits=[tok_x1[j]], inc="S")
                banks[bo].r = tok_cp2[j]
                tok_sq2[j] = em.emit("S", (lambda j=j: lambda s: s.activation(
                    ua1[:, j], ua0[:, j], AF.Square))(), inc="S")
                if j >= 1:
                    mm(2, s_ones[:], ua0[:, j - 1], j - 1 == 0, j - 1 == DK - 1,
                       waits=[tok_cp2[j - 1]], inc=True)
                if j >= 2:
                    mm(3, s_ones[:], ua1[:, j - 2], j - 2 == 0, j - 2 == DK - 1,
                       waits=[tok_sq2[j - 2]], inc=True)
            mm(2, s_ones[:], ua0[:, DK - 1], False, True,
               waits=[tok_cp2[DK - 1]], inc=True)
            for j in (DK - 2, DK - 1):
                mm(3, s_ones[:], ua1[:, j], False, j == DK - 1,
                   waits=[tok_sq2[j]], inc=True)
            tok_m2, tok_r2, _ = ln_chain(2, 3, M2, R2, None, 1)
            dtok = None
            for i in range(16):
                dtok = em.emit("T", (lambda i=i: lambda t: t.matmul(
                    pbank[6][:], s_ones[:], s_xn[:, i % DK, :],
                    start=True, stop=True))(),
                    waits=[banks[6].r] if i == 0 else [],
                    inc="T" if i == 15 else None)
            xn_war[0] = dtok
            tok_wfg, tok_wfu = {}, {}
            for gh in range(2):
                sp = em.alloc_sem("G")
                tok_wfg[gh] = load_w("wfg", gh, gh, c, sem=sp)
                tok_wfu[gh] = load_w("wfu", gh, gh, c, sem=sp)
            tok_wfd = {}
            for j in range(2):
                tok_wfd[j] = load_w("wfd", j, j, c)
            tok_xn2 = [None] * DK
            for f in range(DK):
                sl = f % 2
                em.emit("V", (lambda f=f, sl=sl: lambda v: v.tensor_sub(
                    s_sr[:, sl], s_x1[:, f], s_stt[:, M2]))(), waits=[tok_m2])
                tok_xn2[f] = em.emit("V", (lambda f=f, sl=sl: lambda v: v.tensor_mul(
                    s_xn2[:, f], s_sr[:, sl], s_stt[:, R2]))(),
                    waits=[tok_r2, xn2_war[0]], inc="V")
            r_war[1] = tok_xn2[-1]

            if c == 0:
                flush_wb({"wug_in", "wug_g", "wb", "wc", "wd", "wout"})
            tok_h = [None] * HK
            sg_war = [None, None]
            for jh in range(HK):
                gh, off = jh // 2, 128 * (jh % 2)
                bg_, bu_ = jh % 2, 2 + jh % 2
                for k in range(DK):
                    mm(bg_, wfg_v[:, gh % 2, k, off:off + 128], s_xn2[:, k, :],
                       k == 0, k == DK - 1,
                       waits=([tok_wfu[gh], tok_xn2[k]] if jh == 0 else
                              ([tok_wfu[gh]] if k == 0 else [])),
                       inc=(k == DK - 1))
                slot_used("wfg", gh, banks[bg_].w)
                for k in range(DK):
                    mm(bu_, wfu_v[:, gh % 2, k, off:off + 128], s_xn2[:, k, :],
                       k == 0, k == DK - 1,
                       waits=[tok_wfu[gh]] if k == 0 else [],
                       inc=(k == DK - 1))
                slot_used("wfu", gh, banks[bu_].w)
                if jh % 2 == 1 and gh + 2 < 12:
                    g2 = gh + 2
                    sp = em.alloc_sem("G")
                    tok_wfg[g2] = load_w("wfg", g2, g2, c, sem=sp)
                    tok_wfu[g2] = load_w("wfu", g2, g2, c, sem=sp)
                if st_next is not None and jh == 12:
                    emit_stats1_mms(st_next)
                tok_silu = em.emit("S", (lambda jh=jh, bg_=bg_: lambda s: s.activation(
                    s_gsb[:, jh % 2], pbank[bg_][:], AF.Silu, scale=FFS))(),
                    waits=[banks[bg_].w, sg_war[jh % 2]], inc="S")
                tok_h[jh] = em.emit("V", (lambda jh=jh, bu_=bu_: lambda v: v.scalar_tensor_tensor(
                    hf_v[:, jh, :], pbank[bu_][:], 1.0, s_gsb[:, jh % 2],
                    ALU.mult, ALU.mult))(),
                    waits=[banks[bu_].w, tok_silu], inc="V")
                banks[bg_].r = tok_h[jh]
                banks[bu_].r = tok_h[jh]
                sg_war[jh % 2] = tok_h[jh]
                if st_next is not None and 14 <= jh < 14 + DK:
                    emit_sq1(st_next, jh - 14)
            xn2_war[0] = banks[3].w

            if st_next is not None:
                st_next["wug_i"], st_next["wug_g"] = {}, {}
                for gg in range(2):
                    sp = em.alloc_sem("Q")
                    st_next["wug_i"][gg] = load_w("wug_in", gg, gg, c + 1, sem=sp)
                    st_next["wug_g"][gg] = load_w("wug_g", gg, gg, c + 1, sem=sp)

            tok_x2 = [None] * DK
            for j in range(DK):
                bo = 4 + j % 2
                for k in range(HK):
                    ws = []
                    if k == 0:
                        ws = [tok_wfd[j]]
                    if j == 0:
                        if k == 0:
                            ws = ws + [tok_h[20]]
                        elif k > 20:
                            ws = [tok_h[k]]
                    mm(bo, wview("wfd", j, HK)[:, k, :], hf_v[:, k, :],
                       k == 0, k == HK - 1, waits=ws, inc=(k == HK - 1))
                slot_used("wfd", j, banks[bo].w)
                if j + 2 < DK:
                    tok_wfd[j + 2] = load_w("wfd", j + 2, j + 2, c)
                tok_x2[j] = em.emit("V", (lambda j=j, bo=bo: lambda v: v.scalar_tensor_tensor(
                    s_x2[:, j], pbank[bo][:], FFS * FFS, s_x1[:, j],
                    ALU.mult, ALU.add))(),
                    waits=[banks[bo].w, x2_war[j]], inc="V")
                banks[bo].r = tok_x2[j]
                x2_war[j] = em.dma("Q", (lambda j=j, t0_=t0_: lambda g: g.dma_start(
                    d_y[P * j:P * (j + 1), t0_:t0_ + L], s_x2[:, j]))(),
                    waits=[tok_x2[j]])
                if st_next is not None:
                    if j == 0:
                        emit_stats1_tail(st_next)
                    elif j - 1 < len(FB_SCHED):
                        for task in FB_SCHED[j - 1]:
                            if task[0] == "f":
                                fb_ops(st_next, int(task[1]))
                            else:
                                fb_w(st_next, int(task[1]))

            if st_next is not None:
                st_next["wb"], st_next["wc"], st_next["wd"] = {}, {}, {}
                for j in range(2):
                    sp = em.alloc_sem("Q")
                    st_next["wb"][j] = load_w("wb", j, j, c + 1, sem=sp)
                    st_next["wc"][j] = load_w("wc", j, j, c + 1, sem=sp)
                    st_next["wd"][j] = load_w("wd", j, j, c + 1, sem=sp)
                st_cur = st_next

        em.emit("Q", None,
                forced=[(f"Q{i}", em.cnt[f"Q{i}"]) for i in range(NQ)
                        if em.cnt[f"Q{i}"]])

        with nc.Block() as block:
            @block.tensor
            def _(t):
                em.replay("T", t, semh)

            @block.vector
            def _(v):
                em.replay("V", v, semh)

            @block.scalar
            def _(s):
                em.replay("S", s, semh)

            @block.gpsimd
            def _(g):
                em.replay("G", g, semh)

            @block.sync
            def _(q):
                em.replay("Q", q, semh)

    return nc


def _prep_host(inputs):
    asc = np.ascontiguousarray
    f32 = np.float32

    def units_kj(wT, nk, junit):
        nu = wT.shape[1] // junit
        t = wT.reshape(nk, P, nu, junit)
        return asc(t.transpose(2, 1, 0, 3).reshape(nu, P, nk * junit))

    winT = np.asarray(inputs["L0_attn_in_w"], f32).T
    wgtT = np.asarray(inputs["L0_attn_gate_w"], f32).T
    wbT = np.asarray(inputs["L0_attn_b_w"], f32).T
    wcT = np.asarray(inputs["L0_attn_c_w"], f32).T
    wdT = np.asarray(inputs["L0_attn_d_w"], f32).T
    woutT = np.asarray(inputs["L0_attn_out_w"], f32).T
    wfgT = np.asarray(inputs["L0_ffn_w_gate"], np.int32).T
    wfuT = np.asarray(inputs["L0_ffn_w_up"], np.int32).T
    wfdT = np.asarray(inputs["L0_ffn_w_down"], np.int32).T
    out = dict(
        dw_wug_in=units_kj(winT, DK, 256),
        dw_wug_g=units_kj(wgtT, DK, 256),
        dw_wb=units_kj(wbT, IK, 128),
        dw_wc=units_kj(wcT, IK, 128),
        dw_wd=units_kj(wdT, IK, 128),
        dw_wout=units_kj(woutT, IK, 128),
        dw_wfg=units_kj(wfgT, DK, 256),
        dw_wfu=units_kj(wfuT, DK, 256),
        dw_wfd=units_kj(wfdT, HK, 128),
    )

    def col(v):
        return asc(np.asarray(v, f32).reshape(IK, P).T)

    out["bias"] = asc(np.concatenate([
        col(inputs["L0_attn_in_b"]), col(inputs["L0_attn_gate_b"]),
        col(inputs["L0_attn_b_b"]), col(inputs["L0_attn_c_b"]),
        col(inputs["L0_attn_d_b"]), col(inputs["L0_attn_a_log"])], axis=1))
    out["nw"] = asc(np.asarray(inputs["L0_attn_norm_w"], f32).reshape(DK, P).T)
    out["ob"] = asc(np.asarray(inputs["L0_attn_out_b"], f32).reshape(DK, P).T)
    return out


def kernel(**inputs):
    x = np.asarray(inputs["L0_attn_x"], np.float32)
    B = x.shape[0]
    shared = _prep_host(inputs)
    nc = build_nc(n_chunks=x.shape[1] // L)
    in_maps = []
    for b in range(B):
        m = dict(shared)
        m["x"] = np.ascontiguousarray(x[b].T)
        in_maps.append(m)
    res = run_bass_kernel_spmd(nc, in_maps, core_ids=list(range(B)))
    return np.stack([np.ascontiguousarray(r["y"].T) for r in res.results], axis=0)


# version 26
# speedup vs baseline: 1.0689x; 1.0689x over previous
import contextlib
import numpy as np
import concourse.bass as bass
import concourse.mybir as mybir
from concourse.bass_utils import run_bass_kernel_spmd

F32 = mybir.dt.float32
BF16 = mybir.dt.bfloat16
I32 = mybir.dt.int32
AF = mybir.ActivationFunctionType
ALU = mybir.AluOpType
AX = mybir.AxisListType

P = 128
DM = 1024
INNER = 2048
HID = 3072
DK = DM // P
IK = INNER // P
HK = HID // P
L = 512
FFS = 1.0 / 127.0
EPS_LN = 1e-5
EPS_RMS = 1e-6
NG = 8
NQ = 8

SEM_NAMES = (["T", "V", "S", "Gc"] + [f"G{i}" for i in range(NG)]
             + [f"Q{i}" for i in range(NQ)])


class Em:

    def __init__(self):
        self.streams = {e: [] for e in "TVSGQ"}
        self.cnt = {s: 0 for s in SEM_NAMES}
        self.know = {e: {} for e in "TVSGQ"}
        self.clock = {}
        self.g_rr = 0
        self.q_rr = 0
        self.self_known = {"V": 0, "S": 0}

    def emit(self, stream, fn, waits=(), inc=None, amt=1, forced=(), is_dma=False):
        k = self.know[stream]
        pruned = []

        def merge(sem, val):
            for s2, v2 in self.clock.get((sem, val), {sem: val}).items():
                if k.get(s2, 0) < v2:
                    k[s2] = v2

        if stream in ("V", "S") and fn is not None and not is_dma:
            prev = self.cnt[stream]
            if prev:
                pruned.append((stream, prev))
                merge(stream, prev)
                self.self_known[stream] = prev
            inc, amt = stream, 1
        for w in forced:
            if not w:
                continue
            pruned.append(w)
            merge(*w)
        for w in waits:
            if not w:
                continue
            sem, val = w
            if is_dma and sem == stream and stream in ("V", "S"):
                if val <= self.self_known[stream]:
                    continue
                pruned.append((sem, val))
                self.self_known[stream] = val
                merge(sem, val)
                continue
            if k.get(sem, 0) >= val:
                continue
            pruned.append((sem, val))
            merge(sem, val)
        tok = None
        if inc:
            self.cnt[inc] += amt
            tok = (inc, self.cnt[inc])
            snap = dict(k)
            snap[inc] = self.cnt[inc]
            self.clock[tok] = snap
            if inc == stream and stream in ("T", "V", "S"):
                k[inc] = self.cnt[inc]
        self.streams[stream].append((fn, pruned, tok, amt))
        return tok

    def alloc_sem(self, fam):
        if fam == "G":
            sem = f"G{self.g_rr % NG}"
            self.g_rr += 1
        else:
            sem = f"Q{self.q_rr % NQ}"
            self.q_rr += 1
        return sem

    def dma(self, stream, fn, waits=(), sem=None):
        if sem is None:
            sem = self.alloc_sem("G" if stream == "G" else "Q")
        prev = [(sem, self.cnt[sem])] if self.cnt[sem] else []
        return self.emit(stream, fn, waits, inc=sem, amt=16, forced=prev, is_dma=True)

    def replay(self, stream, eng, semh):
        for fn, waits, tok, amt in self.streams[stream]:
            for sem, val in waits:
                eng.wait_ge(semh[sem], val)
            inst = fn(eng) if fn is not None else None
            if tok is not None:
                inst.then_inc(semh[tok[0]], amt)


class Bank:
    __slots__ = ("w", "r")

    def __init__(self):
        self.w = None
        self.r = None


N_UNITS = {"wug_in": DK, "wug_g": DK, "wb": IK, "wc": IK, "wd": IK,
           "wout": DK, "wfg": 12, "wfu": 12, "wfd": DK}
UNIT_SZ = {r: 2048 for r in N_UNITS}
UNIT_SZ["wfd"] = HK * P
W_DT = {"wfg": I32, "wfu": I32, "wfd": I32}
SLOTS = {"wug_in": [0, 2048],
         "wout": [0, 2048, 4096, 6144],
         "wug_g": [4096, 6144],
         "wfg": [4096, 6144],
         "wb": [8192, 10240],
         "wc": [12288, 14336],
         "wfu": [20480, 22528],
         "wd": [16384, 18432],
         "wfd": [16384, 8192]}


def build_nc(n_chunks=8):
    toks = n_chunks * L
    nc = bass.Bass("TRN2", target_bir_lowering=False)
    d_x = nc.dram_tensor("x", [DM, toks], F32, kind="ExternalInput").ap()
    d_y = nc.dram_tensor("y", [DM, toks], F32, kind="ExternalOutput").ap()
    d_bias = nc.dram_tensor("bias", [P, 96], F32, kind="ExternalInput").ap()
    d_nw = nc.dram_tensor("nw", [P, DK], F32, kind="ExternalInput").ap()
    d_ob = nc.dram_tensor("ob", [P, DK], F32, kind="ExternalInput").ap()

    cw = {r: nc.dram_tensor(f"cw_{r}", [n, P, UNIT_SZ[r]], BF16, kind="Internal").ap()
          for r, n in N_UNITS.items()}
    dw = {r: nc.dram_tensor(f"dw_{r}", [n, P, UNIT_SZ[r]], W_DT.get(r, F32),
                            kind="ExternalInput").ap()
          for r, n in N_UNITS.items()}
    FAM_Q = {"wb": "Q", "wc": "Q", "wd": "Q", "wug_in": "Q", "wug_g": "Q",
             "wout": "G", "wfg": "G", "wfu": "G", "wfd": "G"}

    with contextlib.ExitStack() as ctx:
        e = ctx.enter_context
        s_bias = e(nc.sbuf_tensor("s_bias", [P, 96], F32))
        s_a = e(nc.sbuf_tensor("s_a", [P, IK], F32))
        s_h = e(nc.sbuf_tensor("s_h", [P, IK], F32))
        s_nw = e(nc.sbuf_tensor("s_nw", [P, DK], F32))
        s_ob = e(nc.sbuf_tensor("s_ob", [P, DK], F32))
        s_ones = e(nc.sbuf_tensor("s_ones", [P, P], BF16))
        s_x = e(nc.sbuf_tensor("s_x", [P, DK, L], F32))
        s_xbf = e(nc.sbuf_tensor("s_xbf", [P, DK, L], BF16))
        s_w = e(nc.sbuf_tensor("s_w", [P, DK, L], F32))
        s_x1 = e(nc.sbuf_tensor("s_x1", [P, DK, L], F32))
        s_x2 = e(nc.sbuf_tensor("s_x2", [P, DK, L], F32))
        s_xn = e(nc.sbuf_tensor("s_xn", [P, DK, L], BF16))
        s_xn2 = e(nc.sbuf_tensor("s_xn2", [P, DK, L], BF16))
        s_stt = e(nc.sbuf_tensor("s_stt", [P, 7, L], F32))
        s_sr = e(nc.sbuf_tensor("s_sr", [P, 2, L], F32))
        s_uy = e(nc.sbuf_tensor("s_uy", [P, 2 * IK * L], BF16))
        s_bsb = e(nc.sbuf_tensor("s_bsb", [P, 2, L], F32))
        s_hs = e(nc.sbuf_tensor("s_hs", [P, 2, L], F32))
        s_csb = e(nc.sbuf_tensor("s_csb", [P, 2, L], F32))
        s_yv = e(nc.sbuf_tensor("s_yv", [P, 2, L], F32))
        s_gsb = e(nc.sbuf_tensor("s_gsb", [P, 2, L], F32))
        s_wa = e(nc.sbuf_tensor("s_wa", [P, 24576], BF16))
        pbank = [e(nc.psum_tensor(f"pb{i}", [P, 512], F32)) for i in range(8)]
        semh = {s: e(nc.semaphore(s)) for s in SEM_NAMES}

        u_v = s_uy[:, 0:IK * L].rearrange("p (k t) -> p k t", k=IK)
        y_v = s_uy[:, IK * L:2 * IK * L].rearrange("p (k t) -> p k t", k=IK)
        hf_v = s_uy[:, 0:HK * L].rearrange("p (k t) -> p k t", k=HK)
        ua0 = s_uy[:, 0:DK * L].rearrange("p (k t) -> p k t", k=DK)
        ua1 = s_uy[:, DK * L:2 * DK * L].rearrange("p (k t) -> p k t", k=DK)
        M1, R1, RMS1, M2, R2, T0c, Vc = range(7)

        em = Em()
        banks = [Bank() for _ in range(8)]

        occ = {}
        pending_wb = {}

        def slot_of(role, seq):
            sl = SLOTS[role]
            return (sl[seq % len(sl)], UNIT_SZ[role])

        def slot_flat(slot):
            return s_wa[:, slot[0]:slot[0] + slot[1]]

        def wview(role, seq, kcnt):
            return slot_flat(slot_of(role, seq)).rearrange("p (k j) -> p k j", k=kcnt)

        def slot_used(role, seq, tok):
            occ.setdefault(slot_of(role, seq), {}).setdefault(role, []).append(tok)

        def overlaps(a, b):
            return a[0] < b[0] + b[1] and b[0] < a[0] + a[1]

        def load_w(role, seq, unit, ch, sem=None):
            slot = slot_of(role, seq)
            ws = []
            for oslot in list(occ):
                if not overlaps(oslot, slot):
                    continue
                o = occ[oslot]
                for r2 in list(o):
                    key2 = (r2, oslot)
                    if key2 in pending_wb:
                        wunit, ltok = pending_wb.pop(key2)
                        wtok = em.dma("Q", (lambda r2=r2, wunit=wunit, sf=slot_flat(oslot):
                                            lambda g: g.dma_start(cw[r2][wunit], sf))(),
                                      waits=[ltok])
                        o.setdefault(r2, []).append(wtok)
                for lst in o.values():
                    ws.extend(lst)
                del occ[oslot]
            occ[slot] = {role: []}
            if ch == 0:
                tok = em.dma("G", (lambda dst=slot_flat(slot), role=role, unit=unit:
                                   lambda g: g.dma_start(dst, dw[role][unit]))(),
                             waits=ws, sem=sem)
                pending_wb[(role, slot)] = (unit, tok)
            else:
                tok = em.dma(FAM_Q[role], (lambda dst=slot_flat(slot), role=role,
                                           unit=unit: lambda g: g.dma_start(dst, cw[role][unit]))(),
                             waits=ws, sem=sem)
            return tok

        def flush_wb(roles):
            for key2 in list(pending_wb):
                r2, slot = key2
                if r2 not in roles:
                    continue
                wunit, ltok = pending_wb.pop(key2)
                wtok = em.dma("Q", (lambda r2=r2, wunit=wunit, sf=slot_flat(slot):
                                    lambda g: g.dma_start(cw[r2][wunit], sf))(),
                              waits=[ltok])
                occ.setdefault(slot, {}).setdefault(r2, []).append(wtok)

        tok_bias = em.dma("Q", lambda g: g.dma_start(s_bias[:], d_bias))
        tok_nw = em.dma("Q", lambda g: g.dma_start(s_nw[:], d_nw))
        tok_ob = em.dma("Q", lambda g: g.dma_start(s_ob[:], d_ob))
        tok_ones = em.emit("V", lambda v: v.memset(s_ones[:], 1.0), inc="V")
        tok_a = em.emit("S", lambda s: s.activation(
            s_a[:], s_bias[:, 80:96], AF.Sigmoid, scale=-1.0),
            waits=[tok_bias], inc="S")
        em.emit("V", lambda v: v.memset(s_h[:], 0.0))

        x_war = [() for _ in range(DK)]
        xbf_war = [None]
        xn_war = [None]
        xn2_war = [None]
        x2_war = [None] * DK
        scr_war = [None]
        r_war = [None, None]

        def mm(bank_i, lhsT, rhs, start, stop, waits=(), inc=False):
            b = banks[bank_i]
            ws = list(waits)
            if start:
                ws.append(b.r)
            tok = em.emit(
                "T",
                (lambda bi=bank_i, lh=lhsT, rh=rhs, st=start, sp=stop:
                 lambda t: t.matmul(pbank[bi][:], lh, rh, start=st, stop=sp))(),
                waits=ws, inc="T" if inc else None)
            if stop and inc:
                b.w = tok
            return tok

        def ln_chain(b_sum, b_sq, mc, rc_col, rms_col, ln):
            m_ = s_stt[:, mc]
            t0 = s_stt[:, T0c]
            v_ = s_stt[:, Vc]
            tok_m = em.emit("V", (lambda m_=m_, b=b_sum: lambda v: v.tensor_scalar(
                m_, pbank[b][:], 1.0 / DM, None, ALU.mult))(),
                waits=[banks[b_sum].w, scr_war[0]], inc="V")
            banks[b_sum].r = tok_m
            tok_t0 = em.emit("V", (lambda t0=t0, b=b_sq: lambda v: v.tensor_scalar(
                t0, pbank[b][:], 1.0 / DM, None, ALU.mult))(),
                waits=[banks[b_sq].w], inc="V")
            banks[b_sq].r = tok_t0
            em.emit("V", (lambda v_=v_, m_=m_: lambda v: v.scalar_tensor_tensor(
                v_, m_, -1.0, m_, ALU.mult, ALU.mult))())
            tok_vp = em.emit("V", (lambda v_=v_, t0=t0: lambda v: v.scalar_tensor_tensor(
                v_, t0, EPS_LN, v_, ALU.add, ALU.add))(), inc="V")
            em.emit("S", (lambda t0=t0, v_=v_: lambda s: s.activation(
                t0, v_, AF.Ln))(), waits=[tok_vp])
            tok_r = em.emit("S", (lambda rc_col=rc_col, t0=t0: lambda s: s.activation(
                s_stt[:, rc_col], t0, AF.Exp, scale=-0.5))(),
                waits=[r_war[ln]], inc="S")
            tok_rms = None
            if rms_col is not None:
                tok_rc = em.emit("S", (lambda v_=v_, t0=t0: lambda s: s.activation(
                    v_, t0, AF.Exp, scale=-1.0))(), inc="S")
                tok_q = em.emit("V", (lambda v_=v_: lambda v: v.tensor_scalar(
                    v_, v_, -EPS_LN, 1.0 + EPS_RMS, ALU.mult, ALU.add))(),
                    waits=[tok_rc], inc="V")
                em.emit("S", (lambda t0=t0, v_=v_: lambda s: s.activation(
                    t0, v_, AF.Ln))(), waits=[tok_q])
                tok_rms = em.emit("S", (lambda rms_col=rms_col, t0=t0: lambda s: s.activation(
                    s_stt[:, rms_col], t0, AF.Exp, scale=-0.5))(), inc="S")
            scr_war[0] = tok_rms if tok_rms is not None else tok_r
            return tok_m, tok_r, tok_rms

        def emit_x_copies(c):
            t0_ = c * L
            st = {"c": c, "tok_x": [], "tok_cp": [], "tok_xn": [],
                  "tok_summ": [], "tok_sq": [None] * DK}
            for f in range(DK):
                st["tok_x"].append(em.dma("Q", (lambda f=f, t0_=t0_: lambda g: g.dma_start(
                    s_x[:, f], d_x[P * f:P * (f + 1), t0_:t0_ + L]))(),
                    waits=list(x_war[f])))
            for f in range(DK):
                st["tok_cp"].append(em.emit("V", (lambda f=f: lambda v: v.tensor_copy(
                    s_xbf[:, f], s_x[:, f]))(),
                    waits=[st["tok_x"][f], xbf_war[0]], inc="V"))
            return st

        def emit_stats1_mms(st):
            for f in range(DK):
                st["tok_summ"].append(mm(6, s_ones[:], s_xbf[:, f], f == 0, f == DK - 1,
                                         waits=[st["tok_cp"][f], tok_ones], inc=True))

        def emit_sq1(st, f):
            st["tok_sq"][f] = em.emit("V", (lambda f=f: lambda v: v.tensor_mul(
                s_xbf[:, f], s_xbf[:, f], s_xbf[:, f]))(),
                waits=[st["tok_summ"][f]], inc="V")

        def emit_stats1_tail(st):
            tok_sqm = None
            for f in range(DK):
                tok_sqm = mm(7, s_ones[:], s_xbf[:, f], f == 0, f == DK - 1,
                             waits=[st["tok_sq"][f]], inc=True)
            xbf_war[0] = tok_sqm
            st["ln"] = ln_chain(6, 7, M1, R1, RMS1, 0)

        def fb_ops(st, f):
            tok_m, tok_r, tok_rms = st["ln"]
            sl = f % 2
            em.emit("V", (lambda f=f, sl=sl: lambda v: v.tensor_sub(
                s_sr[:, sl], s_x[:, f], s_stt[:, M1]))(), waits=[tok_m])
            em.emit("V", (lambda sl=sl: lambda v: v.tensor_mul(
                s_sr[:, sl], s_sr[:, sl], s_stt[:, R1]))(), waits=[tok_r])
            tok = em.emit("V", (lambda f=f, sl=sl: lambda v: v.scalar_tensor_tensor(
                s_xn[:, f], s_sr[:, sl], s_nw[:, f:f + 1], s_stt[:, RMS1],
                ALU.mult, ALU.mult))(),
                waits=[tok_rms, tok_nw, xn_war[0]], inc="V")
            st["tok_xn"].append(tok)
            if f == DK - 1:
                r_war[0] = tok

        def fb_w(st, f):
            tok_w = em.emit("V", (lambda f=f, sl=f % 2: lambda v: v.tensor_add(
                s_w[:, f], s_x[:, f], s_sr[:, sl]))(), inc="V")
            x_war[f] = (tok_w,)

        FB_SCHED = [["f0", "f1", "w0"], ["f2", "w1", "f3", "w2"],
                    ["f4", "w3", "f5", "w4"], ["f6", "w5", "f7", "w6", "w7"]]

        st_cur = emit_x_copies(0)
        st_cur["wug_i"], st_cur["wug_g"] = {}, {}
        for gg in range(2):
            st_cur["wug_i"][gg] = load_w("wug_in", gg, gg, 0)
            st_cur["wug_g"][gg] = load_w("wug_g", gg, gg, 0)
        emit_stats1_mms(st_cur)
        for f in range(DK):
            emit_sq1(st_cur, f)
        emit_stats1_tail(st_cur)
        for f in range(DK):
            fb_ops(st_cur, f)
            fb_w(st_cur, f)
        st_cur["wb"], st_cur["wc"], st_cur["wd"] = {}, {}, {}
        for j in range(2):
            st_cur["wb"][j] = load_w("wb", j, j, 0)
            st_cur["wc"][j] = load_w("wc", j, j, 0)
            st_cur["wd"][j] = load_w("wd", j, j, 0)

        wug_in = s_wa[:, 0:4096].rearrange("p (b k j) -> p b k j", b=2, k=DK)
        wug_g = s_wa[:, 4096:8192].rearrange("p (b k j) -> p b k j", b=2, k=DK)
        wfg_v = s_wa[:, 4096:8192].rearrange("p (b k j) -> p b k j", b=2, k=DK)
        wfu_v = s_wa[:, 20480:24576].rearrange("p (b k j) -> p b k j", b=2, k=DK)
        for c in range(n_chunks):
            t0_ = c * L

            tok_wug_i, tok_wug_g = st_cur["wug_i"], st_cur["wug_g"]
            tok_xn = st_cur["tok_xn"]
            tok_u = [None] * IK
            gsb_war = [None, None]
            for jg in range(IK):
                gg, off = jg // 2, 128 * (jg % 2)
                bu, bg = jg % 2, 2 + jg % 2
                for k in range(DK):
                    mm(bu, wug_in[:, gg % 2, k, off:off + 128], s_xn[:, k, :],
                       k == 0, k == DK - 1,
                       waits=([tok_wug_i[gg], tok_xn[k]] if jg == 0 else
                              ([tok_wug_i[gg]] if k == 0 else [])),
                       inc=(k == DK - 1))
                slot_used("wug_in", gg, banks[bu].w)
                for k in range(DK):
                    mm(bg, wug_g[:, gg % 2, k, off:off + 128], s_xn[:, k, :],
                       k == 0, k == DK - 1,
                       waits=[tok_wug_g[gg]] if k == 0 else [],
                       inc=(k == DK - 1))
                slot_used("wug_g", gg, banks[bg].w)
                if jg % 2 == 1 and gg + 2 < DK:
                    g2 = gg + 2
                    tok_wug_i[g2] = load_w("wug_in", g2, g2, c)
                    tok_wug_g[g2] = load_w("wug_g", g2, g2, c)
                tok_sig = em.emit("S", (lambda jg=jg, bg=bg: lambda s: s.activation(
                    s_gsb[:, jg % 2], pbank[bg][:], AF.Sigmoid,
                    bias=s_bias[:, 16 + jg:17 + jg]))(),
                    waits=[banks[bg].w, gsb_war[jg % 2]], inc="S")
                tok_u[jg] = em.emit("V", (lambda jg=jg, bu=bu: lambda v: v.scalar_tensor_tensor(
                    u_v[:, jg, :], pbank[bu][:], s_bias[:, jg:jg + 1], s_gsb[:, jg % 2],
                    ALU.add, ALU.mult))(),
                    waits=[banks[bu].w, tok_sig], inc="V")
                banks[bu].r = tok_u[jg]
                banks[bg].r = tok_u[jg]
                gsb_war[jg % 2] = tok_u[jg]

            st_next = emit_x_copies(c + 1) if c + 1 < n_chunks else None

            tok_wb, tok_wc, tok_wd = st_cur["wb"], st_cur["wc"], st_cur["wd"]
            tok_scan = [None] * IK
            tok_yv = [None] * IK
            tok_y = [None] * IK
            tok_wout = {}
            for j in range(IK):
                bb_, bc_, bd_ = j % 2, 2 + j % 2, 4 + j % 2
                for k in range(IK):
                    mm(bb_, wview("wb", j, IK)[:, k, :], u_v[:, k, :], k == 0, k == IK - 1,
                       waits=[tok_wb[j], tok_u[-1]] if k == 0 else [], inc=(k == IK - 1))
                slot_used("wb", j, banks[bb_].w)
                tok_bsb = em.emit("S", (lambda j=j, bb_=bb_: lambda s: s.activation(
                    s_bsb[:, j % 2], pbank[bb_][:], AF.Identity,
                    bias=s_bias[:, 32 + j:33 + j]))(),
                    waits=[banks[bb_].w, tok_scan[j - 2] if j >= 2 else None], inc="S")
                tok_scan[j] = em.emit("V", (lambda j=j: lambda v: v.tensor_tensor_scan(
                    s_hs[:, j % 2], s_a[:, j:j + 1].broadcast_to((P, L)),
                    s_bsb[:, j % 2], s_h[:, j:j + 1], ALU.mult, ALU.add))(),
                    waits=[tok_bsb, tok_a], inc="V")
                em.emit("V", (lambda j=j: lambda v: v.tensor_copy(
                    s_h[:, j:j + 1], s_hs[:, j % 2, L - 1:L]))())
                for k in range(IK):
                    mm(bc_, wview("wc", j, IK)[:, k, :], u_v[:, k, :], k == 0, k == IK - 1,
                       waits=[tok_wc[j]] if k == 0 else [], inc=(k == IK - 1))
                slot_used("wc", j, banks[bc_].w)
                tok_csb = em.emit("S", (lambda j=j, bc_=bc_: lambda s: s.activation(
                    s_csb[:, j % 2], pbank[bc_][:], AF.Identity,
                    bias=s_bias[:, 48 + j:49 + j]))(),
                    waits=[banks[bc_].w, tok_yv[j - 2] if j >= 2 else None], inc="S")
                for k in range(IK):
                    mm(bd_, wview("wd", j, IK)[:, k, :], u_v[:, k, :], k == 0, k == IK - 1,
                       waits=[tok_wd[j]] if k == 0 else [], inc=(k == IK - 1))
                slot_used("wd", j, banks[bd_].w)
                if j + 2 < IK:
                    j2 = j + 2
                    tok_wb[j2] = load_w("wb", j2, j2, c)
                    tok_wc[j2] = load_w("wc", j2, j2, c)
                    tok_wd[j2] = load_w("wd", j2, j2, c)
                elif j == IK - 2:
                    for sq_ in range(4):
                        tok_wout[sq_] = load_w("wout", sq_, sq_, c)
                tok_yv[j] = em.emit("V", (lambda j=j: lambda v: v.tensor_mul(
                    s_yv[:, j % 2], s_csb[:, j % 2], s_hs[:, j % 2]))(),
                    waits=[tok_csb], inc="V")
                tok_y[j] = em.emit("V", (lambda j=j, bd_=bd_: lambda v: v.scalar_tensor_tensor(
                    y_v[:, j, :], pbank[bd_][:], s_bias[:, 64 + j:65 + j], s_yv[:, j % 2],
                    ALU.add, ALU.add))(),
                    waits=[banks[bd_].w], inc="V")
                banks[bb_].r = tok_y[j]
                banks[bc_].r = tok_y[j]
                banks[bd_].r = tok_y[j]

            tok_x1 = [None] * DK
            tok_cp2 = [None] * DK
            tok_sq2 = [None] * DK
            for j in range(DK):
                bo = j % 2
                for k in range(IK):
                    ws = []
                    if k == 0:
                        ws = [tok_wout[j]]
                    if j == 0:
                        ws = ws + [tok_y[k]]
                    mm(bo, wview("wout", j, IK)[:, k, :], y_v[:, k, :],
                       k == 0, k == IK - 1, waits=ws, inc=(k == IK - 1))
                slot_used("wout", j, banks[bo].w)
                if j + 4 < DK:
                    tok_wout[j + 4] = load_w("wout", j + 4, j + 4, c)
                tok_x1[j] = em.emit("V", (lambda j=j, bo=bo: lambda v: v.scalar_tensor_tensor(
                    s_x1[:, j], pbank[bo][:], s_ob[:, j:j + 1], s_w[:, j],
                    ALU.add, ALU.add))(),
                    waits=[banks[bo].w, tok_ob], inc="V")
                tok_cp2[j] = em.emit("S", (lambda j=j: lambda s: s.activation(
                    ua0[:, j], s_x1[:, j], AF.Identity))(),
                    waits=[tok_x1[j]], inc="S")
                banks[bo].r = tok_cp2[j]
                tok_sq2[j] = em.emit("S", (lambda j=j: lambda s: s.activation(
                    ua1[:, j], ua0[:, j], AF.Square))(), inc="S")
                if j >= 1:
                    mm(2, s_ones[:], ua0[:, j - 1], j - 1 == 0, j - 1 == DK - 1,
                       waits=[tok_cp2[j - 1]], inc=True)
                if j >= 2:
                    mm(3, s_ones[:], ua1[:, j - 2], j - 2 == 0, j - 2 == DK - 1,
                       waits=[tok_sq2[j - 2]], inc=True)
            mm(2, s_ones[:], ua0[:, DK - 1], False, True,
               waits=[tok_cp2[DK - 1]], inc=True)
            for j in (DK - 2, DK - 1):
                mm(3, s_ones[:], ua1[:, j], False, j == DK - 1,
                   waits=[tok_sq2[j]], inc=True)
            tok_m2, tok_r2, _ = ln_chain(2, 3, M2, R2, None, 1)
            dtok = None
            for i in range(16):
                dtok = em.emit("T", (lambda i=i: lambda t: t.matmul(
                    pbank[6][:], s_ones[:], s_xn[:, i % DK, :],
                    start=True, stop=True))(),
                    waits=[banks[6].r] if i == 0 else [],
                    inc="T" if i == 15 else None)
            xn_war[0] = dtok
            tok_wfg, tok_wfu = {}, {}
            for gh in range(2):
                tok_wfg[gh] = load_w("wfg", gh, gh, c)
                tok_wfu[gh] = load_w("wfu", gh, gh, c)
            tok_wfd = {}
            for j in range(2):
                tok_wfd[j] = load_w("wfd", j, j, c)
            tok_xn2 = [None] * DK
            for f in range(DK):
                sl = f % 2
                em.emit("V", (lambda f=f, sl=sl: lambda v: v.tensor_sub(
                    s_sr[:, sl], s_x1[:, f], s_stt[:, M2]))(), waits=[tok_m2])
                tok_xn2[f] = em.emit("V", (lambda f=f, sl=sl: lambda v: v.tensor_mul(
                    s_xn2[:, f], s_sr[:, sl], s_stt[:, R2]))(),
                    waits=[tok_r2, xn2_war[0]], inc="V")
            r_war[1] = tok_xn2[-1]

            if c == 0:
                flush_wb({"wug_in", "wug_g", "wb", "wc", "wd", "wout"})
            tok_h = [None] * HK
            sg_war = [None, None]
            for jh in range(HK):
                gh, off = jh // 2, 128 * (jh % 2)
                bg_, bu_ = jh % 2, 2 + jh % 2
                for k in range(DK):
                    mm(bg_, wfg_v[:, gh % 2, k, off:off + 128], s_xn2[:, k, :],
                       k == 0, k == DK - 1,
                       waits=([tok_wfg[gh], tok_xn2[k]] if jh == 0 else
                              ([tok_wfg[gh]] if k == 0 else [])),
                       inc=(k == DK - 1))
                slot_used("wfg", gh, banks[bg_].w)
                for k in range(DK):
                    mm(bu_, wfu_v[:, gh % 2, k, off:off + 128], s_xn2[:, k, :],
                       k == 0, k == DK - 1,
                       waits=[tok_wfu[gh]] if k == 0 else [],
                       inc=(k == DK - 1))
                slot_used("wfu", gh, banks[bu_].w)
                if jh % 2 == 1 and gh + 2 < 12:
                    g2 = gh + 2
                    tok_wfg[g2] = load_w("wfg", g2, g2, c)
                    tok_wfu[g2] = load_w("wfu", g2, g2, c)
                if st_next is not None and jh == 12:
                    emit_stats1_mms(st_next)
                tok_silu = em.emit("S", (lambda jh=jh, bg_=bg_: lambda s: s.activation(
                    s_gsb[:, jh % 2], pbank[bg_][:], AF.Silu, scale=FFS))(),
                    waits=[banks[bg_].w, sg_war[jh % 2]], inc="S")
                tok_h[jh] = em.emit("V", (lambda jh=jh, bu_=bu_: lambda v: v.scalar_tensor_tensor(
                    hf_v[:, jh, :], pbank[bu_][:], 1.0, s_gsb[:, jh % 2],
                    ALU.mult, ALU.mult))(),
                    waits=[banks[bu_].w, tok_silu], inc="V")
                banks[bg_].r = tok_h[jh]
                banks[bu_].r = tok_h[jh]
                sg_war[jh % 2] = tok_h[jh]
                if st_next is not None and 14 <= jh < 14 + DK:
                    emit_sq1(st_next, jh - 14)
            xn2_war[0] = banks[3].w

            if st_next is not None:
                st_next["wug_i"], st_next["wug_g"] = {}, {}
                for gg in range(2):
                    st_next["wug_i"][gg] = load_w("wug_in", gg, gg, c + 1)
                    st_next["wug_g"][gg] = load_w("wug_g", gg, gg, c + 1)

            tok_x2 = [None] * DK
            for j in range(DK):
                bo = 4 + j % 2
                for k in range(HK):
                    ws = []
                    if k == 0:
                        ws = [tok_wfd[j]]
                    if j == 0:
                        if k == 0:
                            ws = ws + [tok_h[20]]
                        elif k > 20:
                            ws = [tok_h[k]]
                    mm(bo, wview("wfd", j, HK)[:, k, :], hf_v[:, k, :],
                       k == 0, k == HK - 1, waits=ws, inc=(k == HK - 1))
                slot_used("wfd", j, banks[bo].w)
                if j + 2 < DK:
                    tok_wfd[j + 2] = load_w("wfd", j + 2, j + 2, c)
                tok_x2[j] = em.emit("V", (lambda j=j, bo=bo: lambda v: v.scalar_tensor_tensor(
                    s_x2[:, j], pbank[bo][:], FFS * FFS, s_x1[:, j],
                    ALU.mult, ALU.add))(),
                    waits=[banks[bo].w, x2_war[j]], inc="V")
                banks[bo].r = tok_x2[j]
                x2_war[j] = em.dma("Q", (lambda j=j, t0_=t0_: lambda g: g.dma_start(
                    d_y[P * j:P * (j + 1), t0_:t0_ + L], s_x2[:, j]))(),
                    waits=[tok_x2[j]])
                if st_next is not None:
                    if j == 0:
                        emit_stats1_tail(st_next)
                    elif j - 1 < len(FB_SCHED):
                        for task in FB_SCHED[j - 1]:
                            if task[0] == "f":
                                fb_ops(st_next, int(task[1]))
                            else:
                                fb_w(st_next, int(task[1]))

            if st_next is not None:
                st_next["wb"], st_next["wc"], st_next["wd"] = {}, {}, {}
                for j in range(2):
                    st_next["wb"][j] = load_w("wb", j, j, c + 1)
                    st_next["wc"][j] = load_w("wc", j, j, c + 1)
                    st_next["wd"][j] = load_w("wd", j, j, c + 1)
                st_cur = st_next

        em.emit("Q", None,
                forced=[(f"Q{i}", em.cnt[f"Q{i}"]) for i in range(NQ)
                        if em.cnt[f"Q{i}"]])

        with nc.Block() as block:
            @block.tensor
            def _(t):
                em.replay("T", t, semh)

            @block.vector
            def _(v):
                em.replay("V", v, semh)

            @block.scalar
            def _(s):
                em.replay("S", s, semh)

            @block.gpsimd
            def _(g):
                em.replay("G", g, semh)

            @block.sync
            def _(q):
                em.replay("Q", q, semh)

    return nc


def _prep_host(inputs):
    asc = np.ascontiguousarray
    f32 = np.float32

    def units_kj(wT, nk, junit):
        nu = wT.shape[1] // junit
        t = wT.reshape(nk, P, nu, junit)
        return asc(t.transpose(2, 1, 0, 3).reshape(nu, P, nk * junit))

    winT = np.asarray(inputs["L0_attn_in_w"], f32).T
    wgtT = np.asarray(inputs["L0_attn_gate_w"], f32).T
    wbT = np.asarray(inputs["L0_attn_b_w"], f32).T
    wcT = np.asarray(inputs["L0_attn_c_w"], f32).T
    wdT = np.asarray(inputs["L0_attn_d_w"], f32).T
    woutT = np.asarray(inputs["L0_attn_out_w"], f32).T
    wfgT = np.asarray(inputs["L0_ffn_w_gate"], np.int32).T
    wfuT = np.asarray(inputs["L0_ffn_w_up"], np.int32).T
    wfdT = np.asarray(inputs["L0_ffn_w_down"], np.int32).T
    out = dict(
        dw_wug_in=units_kj(winT, DK, 256),
        dw_wug_g=units_kj(wgtT, DK, 256),
        dw_wb=units_kj(wbT, IK, 128),
        dw_wc=units_kj(wcT, IK, 128),
        dw_wd=units_kj(wdT, IK, 128),
        dw_wout=units_kj(woutT, IK, 128),
        dw_wfg=units_kj(wfgT, DK, 256),
        dw_wfu=units_kj(wfuT, DK, 256),
        dw_wfd=units_kj(wfdT, HK, 128),
    )

    def col(v):
        return asc(np.asarray(v, f32).reshape(IK, P).T)

    out["bias"] = asc(np.concatenate([
        col(inputs["L0_attn_in_b"]), col(inputs["L0_attn_gate_b"]),
        col(inputs["L0_attn_b_b"]), col(inputs["L0_attn_c_b"]),
        col(inputs["L0_attn_d_b"]), col(inputs["L0_attn_a_log"])], axis=1))
    out["nw"] = asc(np.asarray(inputs["L0_attn_norm_w"], f32).reshape(DK, P).T)
    out["ob"] = asc(np.asarray(inputs["L0_attn_out_b"], f32).reshape(DK, P).T)
    return out


def kernel(**inputs):
    x = np.asarray(inputs["L0_attn_x"], np.float32)
    B = x.shape[0]
    shared = _prep_host(inputs)
    nc = build_nc(n_chunks=x.shape[1] // L)
    in_maps = []
    for b in range(B):
        m = dict(shared)
        m["x"] = np.ascontiguousarray(x[b].T)
        in_maps.append(m)
    res = run_bass_kernel_spmd(nc, in_maps, core_ids=list(range(B)))
    return np.stack([np.ascontiguousarray(r["y"].T) for r in res.results], axis=0)


# version 27
# speedup vs baseline: 1.0925x; 1.0221x over previous
import contextlib
import numpy as np
import concourse.bass as bass
import concourse.mybir as mybir
from concourse.bass_utils import run_bass_kernel_spmd

F32 = mybir.dt.float32
BF16 = mybir.dt.bfloat16
I32 = mybir.dt.int32
AF = mybir.ActivationFunctionType
ALU = mybir.AluOpType
AX = mybir.AxisListType

P = 128
DM = 1024
INNER = 2048
HID = 3072
DK = DM // P
IK = INNER // P
HK = HID // P
L = 512
FFS = 1.0 / 127.0
EPS_LN = 1e-5
EPS_RMS = 1e-6
NG = 8
NQ = 8

SEM_NAMES = (["T", "V", "S", "Gc"] + [f"G{i}" for i in range(NG)]
             + [f"Q{i}" for i in range(NQ)])


class Em:

    def __init__(self):
        self.streams = {e: [] for e in "TVSGQ"}
        self.cnt = {s: 0 for s in SEM_NAMES}
        self.know = {e: {} for e in "TVSGQ"}
        self.clock = {}
        self.g_rr = 0
        self.q_rr = 0
        self.self_known = {"V": 0, "S": 0}

    def emit(self, stream, fn, waits=(), inc=None, amt=1, forced=(), is_dma=False):
        k = self.know[stream]
        pruned = []

        def merge(sem, val):
            for s2, v2 in self.clock.get((sem, val), {sem: val}).items():
                if k.get(s2, 0) < v2:
                    k[s2] = v2

        if stream in ("V", "S") and fn is not None and not is_dma:
            prev = self.cnt[stream]
            if prev:
                pruned.append((stream, prev))
                merge(stream, prev)
                self.self_known[stream] = prev
            inc, amt = stream, 1
        for w in forced:
            if not w:
                continue
            pruned.append(w)
            merge(*w)
        for w in waits:
            if not w:
                continue
            sem, val = w
            if is_dma and sem == stream and stream in ("V", "S"):
                if val <= self.self_known[stream]:
                    continue
                pruned.append((sem, val))
                self.self_known[stream] = val
                merge(sem, val)
                continue
            if k.get(sem, 0) >= val:
                continue
            pruned.append((sem, val))
            merge(sem, val)
        tok = None
        if inc:
            self.cnt[inc] += amt
            tok = (inc, self.cnt[inc])
            snap = dict(k)
            snap[inc] = self.cnt[inc]
            self.clock[tok] = snap
            if inc == stream and stream in ("T", "V", "S"):
                k[inc] = self.cnt[inc]
        self.streams[stream].append((fn, pruned, tok, amt))
        return tok

    def alloc_sem(self, fam):
        if fam == "G":
            sem = f"G{self.g_rr % NG}"
            self.g_rr += 1
        else:
            sem = f"Q{self.q_rr % NQ}"
            self.q_rr += 1
        return sem

    def dma(self, stream, fn, waits=(), sem=None):
        if sem is None:
            sem = self.alloc_sem("G" if stream == "G" else "Q")
        prev = [(sem, self.cnt[sem])] if self.cnt[sem] else []
        return self.emit(stream, fn, waits, inc=sem, amt=16, forced=prev, is_dma=True)

    def replay(self, stream, eng, semh):
        for fn, waits, tok, amt in self.streams[stream]:
            for sem, val in waits:
                eng.wait_ge(semh[sem], val)
            inst = fn(eng) if fn is not None else None
            if tok is not None:
                inst.then_inc(semh[tok[0]], amt)


class Bank:
    __slots__ = ("w", "r")

    def __init__(self):
        self.w = None
        self.r = None


N_UNITS = {"wug_in": DK, "wug_g": DK, "wb": IK, "wc": IK, "wd": IK,
           "wout": DK, "wfg": 12, "wfu": 12, "wfd": DK}
UNIT_SZ = {r: 2048 for r in N_UNITS}
UNIT_SZ["wfd"] = HK * P
W_DT = {"wfg": I32, "wfu": I32, "wfd": I32}
SLOTS = {"wug_in": [0, 2048],
         "wout": [0, 2048, 4096, 6144],
         "wug_g": [4096, 6144],
         "wfg": [4096, 6144],
         "wb": [8192, 10240],
         "wc": [12288, 14336],
         "wfu": [20480, 22528],
         "wd": [16384, 18432],
         "wfd": [16384, 8192]}


def build_nc(n_chunks=8):
    toks = n_chunks * L
    nc = bass.Bass("TRN2", target_bir_lowering=False)
    d_x = nc.dram_tensor("x", [DM, toks], F32, kind="ExternalInput").ap()
    d_y = nc.dram_tensor("y", [DM, toks], F32, kind="ExternalOutput").ap()
    d_bias = nc.dram_tensor("bias", [P, 96], F32, kind="ExternalInput").ap()
    d_nw = nc.dram_tensor("nw", [P, DK], F32, kind="ExternalInput").ap()
    d_ob = nc.dram_tensor("ob", [P, DK], F32, kind="ExternalInput").ap()

    cw = {r: nc.dram_tensor(f"cw_{r}", [n, P, UNIT_SZ[r]], BF16, kind="Internal").ap()
          for r, n in N_UNITS.items()}
    dw = {r: nc.dram_tensor(f"dw_{r}", [n, P, UNIT_SZ[r]], W_DT.get(r, F32),
                            kind="ExternalInput").ap()
          for r, n in N_UNITS.items()}
    FAM_Q = {"wb": "Q", "wc": "Q", "wd": "Q", "wug_in": "Q", "wug_g": "Q",
             "wout": "G", "wfg": "G", "wfu": "G", "wfd": "G"}

    with contextlib.ExitStack() as ctx:
        e = ctx.enter_context
        s_bias = e(nc.sbuf_tensor("s_bias", [P, 96], F32))
        s_a = e(nc.sbuf_tensor("s_a", [P, IK], F32))
        s_h = e(nc.sbuf_tensor("s_h", [P, IK], F32))
        s_nw = e(nc.sbuf_tensor("s_nw", [P, DK], F32))
        s_ob = e(nc.sbuf_tensor("s_ob", [P, DK], F32))
        s_ones = e(nc.sbuf_tensor("s_ones", [P, P], BF16))
        s_x = e(nc.sbuf_tensor("s_x", [P, DK, L], F32))
        s_xbf = e(nc.sbuf_tensor("s_xbf", [P, DK, L], BF16))
        s_w = e(nc.sbuf_tensor("s_w", [P, DK, L], F32))
        s_x1 = e(nc.sbuf_tensor("s_x1", [P, DK, L], F32))
        s_x2 = e(nc.sbuf_tensor("s_x2", [P, DK, L], F32))
        s_xn = e(nc.sbuf_tensor("s_xn", [P, DK, L], BF16))
        s_xn2 = e(nc.sbuf_tensor("s_xn2", [P, DK, L], BF16))
        s_stt = e(nc.sbuf_tensor("s_stt", [P, 7, L], F32))
        s_sr = e(nc.sbuf_tensor("s_sr", [P, 2, L], F32))
        s_r2b = e(nc.sbuf_tensor("s_r2b", [P, L], BF16))
        s_uy = e(nc.sbuf_tensor("s_uy", [P, 2 * IK * L], BF16))
        s_bsb = e(nc.sbuf_tensor("s_bsb", [P, 2, L], F32))
        s_hs = e(nc.sbuf_tensor("s_hs", [P, 2, L], F32))
        s_csb = e(nc.sbuf_tensor("s_csb", [P, 2, L], F32))
        s_yv = e(nc.sbuf_tensor("s_yv", [P, 2, L], F32))
        s_gsb = e(nc.sbuf_tensor("s_gsb", [P, 2, L], F32))
        s_wa = e(nc.sbuf_tensor("s_wa", [P, 24576], BF16))
        pbank = [e(nc.psum_tensor(f"pb{i}", [P, 512], F32)) for i in range(8)]
        semh = {s: e(nc.semaphore(s)) for s in SEM_NAMES}

        u_v = s_uy[:, 0:IK * L].rearrange("p (k t) -> p k t", k=IK)
        y_v = s_uy[:, IK * L:2 * IK * L].rearrange("p (k t) -> p k t", k=IK)
        hf_v = s_uy[:, 0:HK * L].rearrange("p (k t) -> p k t", k=HK)
        ua0 = s_uy[:, 0:DK * L].rearrange("p (k t) -> p k t", k=DK)
        ua1 = s_uy[:, DK * L:2 * DK * L].rearrange("p (k t) -> p k t", k=DK)
        M1, R1, RMS1, M2, R2, T0c, Vc = range(7)

        em = Em()
        banks = [Bank() for _ in range(8)]

        occ = {}
        pending_wb = {}

        def slot_of(role, seq):
            sl = SLOTS[role]
            return (sl[seq % len(sl)], UNIT_SZ[role])

        def slot_flat(slot):
            return s_wa[:, slot[0]:slot[0] + slot[1]]

        def wview(role, seq, kcnt):
            return slot_flat(slot_of(role, seq)).rearrange("p (k j) -> p k j", k=kcnt)

        def slot_used(role, seq, tok):
            occ.setdefault(slot_of(role, seq), {}).setdefault(role, []).append(tok)

        def overlaps(a, b):
            return a[0] < b[0] + b[1] and b[0] < a[0] + a[1]

        def load_w(role, seq, unit, ch, sem=None):
            slot = slot_of(role, seq)
            ws = []
            for oslot in list(occ):
                if not overlaps(oslot, slot):
                    continue
                o = occ[oslot]
                for r2 in list(o):
                    key2 = (r2, oslot)
                    if key2 in pending_wb:
                        wunit, ltok = pending_wb.pop(key2)
                        wtok = em.dma("Q", (lambda r2=r2, wunit=wunit, sf=slot_flat(oslot):
                                            lambda g: g.dma_start(cw[r2][wunit], sf))(),
                                      waits=[ltok])
                        o.setdefault(r2, []).append(wtok)
                for lst in o.values():
                    ws.extend(lst)
                del occ[oslot]
            occ[slot] = {role: []}
            if ch == 0:
                tok = em.dma("G", (lambda dst=slot_flat(slot), role=role, unit=unit:
                                   lambda g: g.dma_start(dst, dw[role][unit]))(),
                             waits=ws, sem=sem)
                pending_wb[(role, slot)] = (unit, tok)
            else:
                tok = em.dma(FAM_Q[role], (lambda dst=slot_flat(slot), role=role,
                                           unit=unit: lambda g: g.dma_start(dst, cw[role][unit]))(),
                             waits=ws, sem=sem)
            return tok

        def flush_wb(roles):
            for key2 in list(pending_wb):
                r2, slot = key2
                if r2 not in roles:
                    continue
                wunit, ltok = pending_wb.pop(key2)
                wtok = em.dma("Q", (lambda r2=r2, wunit=wunit, sf=slot_flat(slot):
                                    lambda g: g.dma_start(cw[r2][wunit], sf))(),
                              waits=[ltok])
                occ.setdefault(slot, {}).setdefault(r2, []).append(wtok)

        tok_bias = em.dma("Q", lambda g: g.dma_start(s_bias[:], d_bias))
        tok_nw = em.dma("Q", lambda g: g.dma_start(s_nw[:], d_nw))
        tok_ob = em.dma("Q", lambda g: g.dma_start(s_ob[:], d_ob))
        tok_ones = em.emit("V", lambda v: v.memset(s_ones[:], 1.0), inc="V")
        tok_a = em.emit("S", lambda s: s.activation(
            s_a[:], s_bias[:, 80:96], AF.Sigmoid, scale=-1.0),
            waits=[tok_bias], inc="S")
        em.emit("V", lambda v: v.memset(s_h[:], 0.0))

        x_war = [() for _ in range(DK)]
        xbf_war = [None]
        xn_war = [None]
        xn2_war = [None]
        x2_war = [None] * DK
        scr_war = [None]
        r_war = [None, None]

        def mm(bank_i, lhsT, rhs, start, stop, waits=(), inc=False):
            b = banks[bank_i]
            ws = list(waits)
            if start:
                ws.append(b.r)
            tok = em.emit(
                "T",
                (lambda bi=bank_i, lh=lhsT, rh=rhs, st=start, sp=stop:
                 lambda t: t.matmul(pbank[bi][:], lh, rh, start=st, stop=sp))(),
                waits=ws, inc="T" if inc else None)
            if stop and inc:
                b.w = tok
            return tok

        def ln_chain(b_sum, b_sq, mc, rc_col, rms_col, ln, rc_out=None):
            m_ = s_stt[:, mc]
            t0 = s_stt[:, T0c]
            v_ = s_stt[:, Vc]
            tok_m = em.emit("V", (lambda m_=m_, b=b_sum: lambda v: v.tensor_scalar(
                m_, pbank[b][:], 1.0 / DM, None, ALU.mult))(),
                waits=[banks[b_sum].w, scr_war[0]], inc="V")
            banks[b_sum].r = tok_m
            tok_t0 = em.emit("V", (lambda t0=t0, b=b_sq: lambda v: v.tensor_scalar(
                t0, pbank[b][:], 1.0 / DM, None, ALU.mult))(),
                waits=[banks[b_sq].w], inc="V")
            banks[b_sq].r = tok_t0
            em.emit("V", (lambda v_=v_, m_=m_: lambda v: v.scalar_tensor_tensor(
                v_, m_, -1.0, m_, ALU.mult, ALU.mult))())
            tok_vp = em.emit("V", (lambda v_=v_, t0=t0: lambda v: v.scalar_tensor_tensor(
                v_, t0, EPS_LN, v_, ALU.add, ALU.add))(), inc="V")
            em.emit("S", (lambda t0=t0, v_=v_: lambda s: s.activation(
                t0, v_, AF.Ln))(), waits=[tok_vp])
            rdst = s_stt[:, rc_col] if rc_out is None else rc_out
            tok_r = em.emit("S", (lambda rdst=rdst, t0=t0: lambda s: s.activation(
                rdst, t0, AF.Exp, scale=-0.5))(),
                waits=[r_war[ln]], inc="S")
            tok_rms = None
            if rms_col is not None:
                tok_rc = em.emit("S", (lambda v_=v_, t0=t0: lambda s: s.activation(
                    v_, t0, AF.Exp, scale=-1.0))(), inc="S")
                tok_q = em.emit("V", (lambda v_=v_: lambda v: v.tensor_scalar(
                    v_, v_, -EPS_LN, 1.0 + EPS_RMS, ALU.mult, ALU.add))(),
                    waits=[tok_rc], inc="V")
                em.emit("S", (lambda t0=t0, v_=v_: lambda s: s.activation(
                    t0, v_, AF.Ln))(), waits=[tok_q])
                tok_rms = em.emit("S", (lambda rms_col=rms_col, t0=t0: lambda s: s.activation(
                    s_stt[:, rms_col], t0, AF.Exp, scale=-0.5))(), inc="S")
            scr_war[0] = tok_rms if tok_rms is not None else tok_r
            return tok_m, tok_r, tok_rms

        def emit_x_copies(c):
            t0_ = c * L
            st = {"c": c, "tok_x": [], "tok_cp": [], "tok_xn": [],
                  "tok_summ": [], "tok_sq": [None] * DK}
            for f in range(DK):
                st["tok_x"].append(em.dma("Q", (lambda f=f, t0_=t0_: lambda g: g.dma_start(
                    s_x[:, f], d_x[P * f:P * (f + 1), t0_:t0_ + L]))(),
                    waits=list(x_war[f])))
            for f in range(DK):
                st["tok_cp"].append(em.emit("V", (lambda f=f: lambda v: v.tensor_copy(
                    s_xbf[:, f], s_x[:, f]))(),
                    waits=[st["tok_x"][f], xbf_war[0]], inc="V"))
            return st

        def emit_stats1_mms(st):
            for f in range(DK):
                st["tok_summ"].append(mm(6, s_ones[:], s_xbf[:, f], f == 0, f == DK - 1,
                                         waits=[st["tok_cp"][f], tok_ones], inc=True))

        def emit_sq1(st, f):
            st["tok_sq"][f] = em.emit("V", (lambda f=f: lambda v: v.tensor_mul(
                s_xbf[:, f], s_xbf[:, f], s_xbf[:, f]))(),
                waits=[st["tok_summ"][f]], inc="V")

        def emit_stats1_tail(st):
            tok_sqm = None
            for f in range(DK):
                tok_sqm = mm(7, s_ones[:], s_xbf[:, f], f == 0, f == DK - 1,
                             waits=[st["tok_sq"][f]], inc=True)
            xbf_war[0] = tok_sqm
            st["ln"] = ln_chain(6, 7, M1, R1, RMS1, 0)

        def fb_ops(st, f):
            tok_m, tok_r, tok_rms = st["ln"]
            sl = f % 2
            em.emit("V", (lambda f=f, sl=sl: lambda v: v.tensor_sub(
                s_sr[:, sl], s_x[:, f], s_stt[:, M1]))(), waits=[tok_m])
            em.emit("V", (lambda sl=sl: lambda v: v.tensor_mul(
                s_sr[:, sl], s_sr[:, sl], s_stt[:, R1]))(), waits=[tok_r])
            tok = em.emit("V", (lambda f=f, sl=sl: lambda v: v.scalar_tensor_tensor(
                s_xn[:, f], s_sr[:, sl], s_nw[:, f:f + 1], s_stt[:, RMS1],
                ALU.mult, ALU.mult))(),
                waits=[tok_rms, tok_nw, xn_war[0]], inc="V")
            st["tok_xn"].append(tok)
            if f == DK - 1:
                r_war[0] = tok

        def fb_w(st, f):
            tok_w = em.emit("V", (lambda f=f, sl=f % 2: lambda v: v.tensor_add(
                s_w[:, f], s_x[:, f], s_sr[:, sl]))(), inc="V")
            x_war[f] = (tok_w,)

        FB_SCHED = [["f0", "f1", "w0"], ["f2", "w1", "f3", "w2"],
                    ["f4", "w3", "f5", "w4"], ["f6", "w5", "f7", "w6", "w7"]]

        st_cur = emit_x_copies(0)
        st_cur["wug_i"], st_cur["wug_g"] = {}, {}
        for gg in range(2):
            st_cur["wug_i"][gg] = load_w("wug_in", gg, gg, 0)
            st_cur["wug_g"][gg] = load_w("wug_g", gg, gg, 0)
        emit_stats1_mms(st_cur)
        for f in range(DK):
            emit_sq1(st_cur, f)
        emit_stats1_tail(st_cur)
        for f in range(DK):
            fb_ops(st_cur, f)
            fb_w(st_cur, f)
        st_cur["wb"], st_cur["wc"], st_cur["wd"] = {}, {}, {}
        for j in range(2):
            st_cur["wb"][j] = load_w("wb", j, j, 0)
            st_cur["wc"][j] = load_w("wc", j, j, 0)
            st_cur["wd"][j] = load_w("wd", j, j, 0)

        wug_in = s_wa[:, 0:4096].rearrange("p (b k j) -> p b k j", b=2, k=DK)
        wug_g = s_wa[:, 4096:8192].rearrange("p (b k j) -> p b k j", b=2, k=DK)
        wfg_v = s_wa[:, 4096:8192].rearrange("p (b k j) -> p b k j", b=2, k=DK)
        wfu_v = s_wa[:, 20480:24576].rearrange("p (b k j) -> p b k j", b=2, k=DK)
        for c in range(n_chunks):
            t0_ = c * L

            tok_wug_i, tok_wug_g = st_cur["wug_i"], st_cur["wug_g"]
            tok_xn = st_cur["tok_xn"]
            tok_u = [None] * IK
            gsb_war = [None, None]
            for jg in range(IK):
                gg, off = jg // 2, 128 * (jg % 2)
                bu, bg = jg % 2, 2 + jg % 2
                for k in range(DK):
                    mm(bu, wug_in[:, gg % 2, k, off:off + 128], s_xn[:, k, :],
                       k == 0, k == DK - 1,
                       waits=([tok_wug_i[gg], tok_xn[k]] if jg == 0 else
                              ([tok_wug_i[gg]] if k == 0 else [])),
                       inc=(k == DK - 1))
                slot_used("wug_in", gg, banks[bu].w)
                for k in range(DK):
                    mm(bg, wug_g[:, gg % 2, k, off:off + 128], s_xn[:, k, :],
                       k == 0, k == DK - 1,
                       waits=[tok_wug_g[gg]] if k == 0 else [],
                       inc=(k == DK - 1))
                slot_used("wug_g", gg, banks[bg].w)
                if jg % 2 == 1 and gg + 2 < DK:
                    g2 = gg + 2
                    tok_wug_i[g2] = load_w("wug_in", g2, g2, c)
                    tok_wug_g[g2] = load_w("wug_g", g2, g2, c)
                tok_sig = em.emit("S", (lambda jg=jg, bg=bg: lambda s: s.activation(
                    s_gsb[:, jg % 2], pbank[bg][:], AF.Sigmoid,
                    bias=s_bias[:, 16 + jg:17 + jg]))(),
                    waits=[banks[bg].w, gsb_war[jg % 2]], inc="S")
                tok_u[jg] = em.emit("V", (lambda jg=jg, bu=bu: lambda v: v.scalar_tensor_tensor(
                    u_v[:, jg, :], pbank[bu][:], s_bias[:, jg:jg + 1], s_gsb[:, jg % 2],
                    ALU.add, ALU.mult))(),
                    waits=[banks[bu].w, tok_sig], inc="V")
                banks[bu].r = tok_u[jg]
                banks[bg].r = tok_u[jg]
                gsb_war[jg % 2] = tok_u[jg]

            st_next = emit_x_copies(c + 1) if c + 1 < n_chunks else None

            tok_wb, tok_wc, tok_wd = st_cur["wb"], st_cur["wc"], st_cur["wd"]
            tok_scan = [None] * IK
            tok_yv = [None] * IK
            tok_y = [None] * IK
            tok_wout = {}
            for j in range(IK):
                bb_, bc_, bd_ = j % 2, 2 + j % 2, 4 + j % 2
                for k in range(IK):
                    mm(bb_, wview("wb", j, IK)[:, k, :], u_v[:, k, :], k == 0, k == IK - 1,
                       waits=[tok_wb[j], tok_u[-1]] if k == 0 else [], inc=(k == IK - 1))
                slot_used("wb", j, banks[bb_].w)
                tok_bsb = em.emit("S", (lambda j=j, bb_=bb_: lambda s: s.activation(
                    s_bsb[:, j % 2], pbank[bb_][:], AF.Identity,
                    bias=s_bias[:, 32 + j:33 + j]))(),
                    waits=[banks[bb_].w, tok_scan[j - 2] if j >= 2 else None], inc="S")
                tok_scan[j] = em.emit("V", (lambda j=j: lambda v: v.tensor_tensor_scan(
                    s_hs[:, j % 2], s_a[:, j:j + 1].broadcast_to((P, L)),
                    s_bsb[:, j % 2], s_h[:, j:j + 1], ALU.mult, ALU.add))(),
                    waits=[tok_bsb, tok_a], inc="V")
                em.emit("V", (lambda j=j: lambda v: v.tensor_copy(
                    s_h[:, j:j + 1], s_hs[:, j % 2, L - 1:L]))())
                for k in range(IK):
                    mm(bc_, wview("wc", j, IK)[:, k, :], u_v[:, k, :], k == 0, k == IK - 1,
                       waits=[tok_wc[j]] if k == 0 else [], inc=(k == IK - 1))
                slot_used("wc", j, banks[bc_].w)
                tok_csb = em.emit("S", (lambda j=j, bc_=bc_: lambda s: s.activation(
                    s_csb[:, j % 2], pbank[bc_][:], AF.Identity,
                    bias=s_bias[:, 48 + j:49 + j]))(),
                    waits=[banks[bc_].w, tok_yv[j - 2] if j >= 2 else None], inc="S")
                for k in range(IK):
                    mm(bd_, wview("wd", j, IK)[:, k, :], u_v[:, k, :], k == 0, k == IK - 1,
                       waits=[tok_wd[j]] if k == 0 else [], inc=(k == IK - 1))
                slot_used("wd", j, banks[bd_].w)
                if j + 2 < IK:
                    j2 = j + 2
                    tok_wb[j2] = load_w("wb", j2, j2, c)
                    tok_wc[j2] = load_w("wc", j2, j2, c)
                    tok_wd[j2] = load_w("wd", j2, j2, c)
                elif j == IK - 2:
                    for sq_ in range(4):
                        tok_wout[sq_] = load_w("wout", sq_, sq_, c)
                tok_yv[j] = em.emit("V", (lambda j=j: lambda v: v.tensor_mul(
                    s_yv[:, j % 2], s_csb[:, j % 2], s_hs[:, j % 2]))(),
                    waits=[tok_csb], inc="V")
                tok_y[j] = em.emit("V", (lambda j=j, bd_=bd_: lambda v: v.scalar_tensor_tensor(
                    y_v[:, j, :], pbank[bd_][:], s_bias[:, 64 + j:65 + j], s_yv[:, j % 2],
                    ALU.add, ALU.add))(),
                    waits=[banks[bd_].w], inc="V")
                banks[bb_].r = tok_y[j]
                banks[bc_].r = tok_y[j]
                banks[bd_].r = tok_y[j]

            tok_x1 = [None] * DK
            tok_cp2 = [None] * DK
            tok_sq2 = [None] * DK
            for j in range(DK):
                bo = j % 2
                for k in range(IK):
                    ws = []
                    if k == 0:
                        ws = [tok_wout[j]]
                    if j == 0:
                        ws = ws + [tok_y[k]]
                    mm(bo, wview("wout", j, IK)[:, k, :], y_v[:, k, :],
                       k == 0, k == IK - 1, waits=ws, inc=(k == IK - 1))
                slot_used("wout", j, banks[bo].w)
                if j + 4 < DK:
                    tok_wout[j + 4] = load_w("wout", j + 4, j + 4, c)
                tok_x1[j] = em.emit("V", (lambda j=j, bo=bo: lambda v: v.scalar_tensor_tensor(
                    s_x1[:, j], pbank[bo][:], s_ob[:, j:j + 1], s_w[:, j],
                    ALU.add, ALU.add))(),
                    waits=[banks[bo].w, tok_ob], inc="V")
                tok_cp2[j] = em.emit("S", (lambda j=j: lambda s: s.activation(
                    ua0[:, j], s_x1[:, j], AF.Identity))(),
                    waits=[tok_x1[j]], inc="S")
                banks[bo].r = tok_cp2[j]
                tok_sq2[j] = em.emit("S", (lambda j=j: lambda s: s.activation(
                    ua1[:, j], ua0[:, j], AF.Square))(), inc="S")
                if j >= 1:
                    mm(2, s_ones[:], ua0[:, j - 1], j - 1 == 0, j - 1 == DK - 1,
                       waits=[tok_cp2[j - 1]], inc=True)
                if j >= 2:
                    mm(3, s_ones[:], ua1[:, j - 2], j - 2 == 0, j - 2 == DK - 1,
                       waits=[tok_sq2[j - 2]], inc=True)
            mm(2, s_ones[:], ua0[:, DK - 1], False, True,
               waits=[tok_cp2[DK - 1]], inc=True)
            for j in (DK - 2, DK - 1):
                mm(3, s_ones[:], ua1[:, j], False, j == DK - 1,
                   waits=[tok_sq2[j]], inc=True)
            tok_m2, tok_r2, _ = ln_chain(2, 3, M2, R2, None, 1, rc_out=s_r2b[:])
            dtok = None
            for i in range(16):
                dtok = em.emit("T", (lambda i=i: lambda t: t.matmul(
                    pbank[6][:], s_ones[:], s_xn[:, i % DK, :],
                    start=True, stop=True))(),
                    waits=[banks[6].r] if i == 0 else [],
                    inc="T" if i == 15 else None)
            xn_war[0] = dtok
            tok_wfg, tok_wfu = {}, {}
            for gh in range(2):
                tok_wfg[gh] = load_w("wfg", gh, gh, c)
                tok_wfu[gh] = load_w("wfu", gh, gh, c)
            tok_wfd = {}
            for j in range(2):
                tok_wfd[j] = load_w("wfd", j, j, c)
            m2_b = s_stt[:, M2:M2 + 1].broadcast_to((P, DK, L))
            r2_b = s_r2b[:].rearrange("p (o t) -> p o t", o=1).broadcast_to((P, DK, L))
            em.emit("V", (lambda m2_b=m2_b: lambda v: v.tensor_sub(
                s_xn2[:], s_x1[:], m2_b))(), waits=[tok_m2, xn2_war[0]])
            tok_xn2w = em.emit("V", (lambda r2_b=r2_b: lambda v: v.tensor_mul(
                s_xn2[:], s_xn2[:], r2_b))(), waits=[tok_r2], inc="V")
            r_war[1] = tok_xn2w

            if c == 0:
                flush_wb({"wug_in", "wug_g", "wb", "wc", "wd", "wout"})
            tok_h = [None] * HK
            sg_war = [None, None]
            for jh in range(HK):
                gh, off = jh // 2, 128 * (jh % 2)
                bg_, bu_ = jh % 2, 2 + jh % 2
                for k in range(DK):
                    mm(bg_, wfg_v[:, gh % 2, k, off:off + 128], s_xn2[:, k, :],
                       k == 0, k == DK - 1,
                       waits=([tok_wfg[gh], tok_xn2w] if jh == 0 and k == 0 else
                              ([tok_wfg[gh]] if k == 0 else [])),
                       inc=(k == DK - 1))
                slot_used("wfg", gh, banks[bg_].w)
                for k in range(DK):
                    mm(bu_, wfu_v[:, gh % 2, k, off:off + 128], s_xn2[:, k, :],
                       k == 0, k == DK - 1,
                       waits=[tok_wfu[gh]] if k == 0 else [],
                       inc=(k == DK - 1))
                slot_used("wfu", gh, banks[bu_].w)
                if jh % 2 == 1 and gh + 2 < 12:
                    g2 = gh + 2
                    tok_wfg[g2] = load_w("wfg", g2, g2, c)
                    tok_wfu[g2] = load_w("wfu", g2, g2, c)
                if st_next is not None and jh == 12:
                    emit_stats1_mms(st_next)
                tok_silu = em.emit("S", (lambda jh=jh, bg_=bg_: lambda s: s.activation(
                    s_gsb[:, jh % 2], pbank[bg_][:], AF.Silu, scale=FFS))(),
                    waits=[banks[bg_].w, sg_war[jh % 2]], inc="S")
                tok_h[jh] = em.emit("V", (lambda jh=jh, bu_=bu_: lambda v: v.scalar_tensor_tensor(
                    hf_v[:, jh, :], pbank[bu_][:], 1.0, s_gsb[:, jh % 2],
                    ALU.mult, ALU.mult))(),
                    waits=[banks[bu_].w, tok_silu], inc="V")
                banks[bg_].r = tok_h[jh]
                banks[bu_].r = tok_h[jh]
                sg_war[jh % 2] = tok_h[jh]
                if st_next is not None and 14 <= jh < 14 + DK:
                    emit_sq1(st_next, jh - 14)
            xn2_war[0] = banks[3].w

            if st_next is not None:
                st_next["wug_i"], st_next["wug_g"] = {}, {}
                for gg in range(2):
                    st_next["wug_i"][gg] = load_w("wug_in", gg, gg, c + 1)
                    st_next["wug_g"][gg] = load_w("wug_g", gg, gg, c + 1)

            tok_x2 = [None] * DK
            for j in range(DK):
                bo = 4 + j % 2
                for k in range(HK):
                    ws = []
                    if k == 0:
                        ws = [tok_wfd[j]]
                    if j == 0:
                        if k == 0:
                            ws = ws + [tok_h[20]]
                        elif k > 20:
                            ws = [tok_h[k]]
                    mm(bo, wview("wfd", j, HK)[:, k, :], hf_v[:, k, :],
                       k == 0, k == HK - 1, waits=ws, inc=(k == HK - 1))
                slot_used("wfd", j, banks[bo].w)
                if j + 2 < DK:
                    tok_wfd[j + 2] = load_w("wfd", j + 2, j + 2, c)
                tok_x2[j] = em.emit("V", (lambda j=j, bo=bo: lambda v: v.scalar_tensor_tensor(
                    s_x2[:, j], pbank[bo][:], FFS * FFS, s_x1[:, j],
                    ALU.mult, ALU.add))(),
                    waits=[banks[bo].w, x2_war[j]], inc="V")
                banks[bo].r = tok_x2[j]
                x2_war[j] = em.dma("Q", (lambda j=j, t0_=t0_: lambda g: g.dma_start(
                    d_y[P * j:P * (j + 1), t0_:t0_ + L], s_x2[:, j]))(),
                    waits=[tok_x2[j]])
                if st_next is not None:
                    if j == 0:
                        emit_stats1_tail(st_next)
                    elif j - 1 < len(FB_SCHED):
                        for task in FB_SCHED[j - 1]:
                            if task[0] == "f":
                                fb_ops(st_next, int(task[1]))
                            else:
                                fb_w(st_next, int(task[1]))

            if st_next is not None:
                st_next["wb"], st_next["wc"], st_next["wd"] = {}, {}, {}
                for j in range(2):
                    st_next["wb"][j] = load_w("wb", j, j, c + 1)
                    st_next["wc"][j] = load_w("wc", j, j, c + 1)
                    st_next["wd"][j] = load_w("wd", j, j, c + 1)
                st_cur = st_next

        em.emit("Q", None,
                forced=[(f"Q{i}", em.cnt[f"Q{i}"]) for i in range(NQ)
                        if em.cnt[f"Q{i}"]])

        with nc.Block() as block:
            @block.tensor
            def _(t):
                em.replay("T", t, semh)

            @block.vector
            def _(v):
                em.replay("V", v, semh)

            @block.scalar
            def _(s):
                em.replay("S", s, semh)

            @block.gpsimd
            def _(g):
                em.replay("G", g, semh)

            @block.sync
            def _(q):
                em.replay("Q", q, semh)

    return nc


def _prep_host(inputs):
    asc = np.ascontiguousarray
    f32 = np.float32

    def units_kj(wT, nk, junit):
        nu = wT.shape[1] // junit
        t = wT.reshape(nk, P, nu, junit)
        return asc(t.transpose(2, 1, 0, 3).reshape(nu, P, nk * junit))

    winT = np.asarray(inputs["L0_attn_in_w"], f32).T
    wgtT = np.asarray(inputs["L0_attn_gate_w"], f32).T
    wbT = np.asarray(inputs["L0_attn_b_w"], f32).T
    wcT = np.asarray(inputs["L0_attn_c_w"], f32).T
    wdT = np.asarray(inputs["L0_attn_d_w"], f32).T
    woutT = np.asarray(inputs["L0_attn_out_w"], f32).T
    wfgT = np.asarray(inputs["L0_ffn_w_gate"], np.int32).T
    wfuT = np.asarray(inputs["L0_ffn_w_up"], np.int32).T
    wfdT = np.asarray(inputs["L0_ffn_w_down"], np.int32).T
    out = dict(
        dw_wug_in=units_kj(winT, DK, 256),
        dw_wug_g=units_kj(wgtT, DK, 256),
        dw_wb=units_kj(wbT, IK, 128),
        dw_wc=units_kj(wcT, IK, 128),
        dw_wd=units_kj(wdT, IK, 128),
        dw_wout=units_kj(woutT, IK, 128),
        dw_wfg=units_kj(wfgT, DK, 256),
        dw_wfu=units_kj(wfuT, DK, 256),
        dw_wfd=units_kj(wfdT, HK, 128),
    )

    def col(v):
        return asc(np.asarray(v, f32).reshape(IK, P).T)

    out["bias"] = asc(np.concatenate([
        col(inputs["L0_attn_in_b"]), col(inputs["L0_attn_gate_b"]),
        col(inputs["L0_attn_b_b"]), col(inputs["L0_attn_c_b"]),
        col(inputs["L0_attn_d_b"]), col(inputs["L0_attn_a_log"])], axis=1))
    out["nw"] = asc(np.asarray(inputs["L0_attn_norm_w"], f32).reshape(DK, P).T)
    out["ob"] = asc(np.asarray(inputs["L0_attn_out_b"], f32).reshape(DK, P).T)
    return out


def kernel(**inputs):
    x = np.asarray(inputs["L0_attn_x"], np.float32)
    B = x.shape[0]
    shared = _prep_host(inputs)
    nc = build_nc(n_chunks=x.shape[1] // L)
    in_maps = []
    for b in range(B):
        m = dict(shared)
        m["x"] = np.ascontiguousarray(x[b].T)
        in_maps.append(m)
    res = run_bass_kernel_spmd(nc, in_maps, core_ids=list(range(B)))
    return np.stack([np.ascontiguousarray(r["y"].T) for r in res.results], axis=0)


# version 28
# speedup vs baseline: 1.1139x; 1.0195x over previous
import contextlib
import numpy as np
import concourse.bass as bass
import concourse.mybir as mybir
from concourse.bass_utils import run_bass_kernel_spmd

F32 = mybir.dt.float32
BF16 = mybir.dt.bfloat16
I32 = mybir.dt.int32
AF = mybir.ActivationFunctionType
ALU = mybir.AluOpType
AX = mybir.AxisListType

P = 128
DM = 1024
INNER = 2048
HID = 3072
DK = DM // P
IK = INNER // P
HK = HID // P
L = 512
FFS = 1.0 / 127.0
EPS_LN = 1e-5
EPS_RMS = 1e-6
NG = 8
NQ = 8

SEM_NAMES = (["T", "V", "S", "Gc"] + [f"G{i}" for i in range(NG)]
             + [f"Q{i}" for i in range(NQ)])


class Em:

    def __init__(self):
        self.streams = {e: [] for e in "TVSGQ"}
        self.cnt = {s: 0 for s in SEM_NAMES}
        self.know = {e: {} for e in "TVSGQ"}
        self.clock = {}
        self.g_rr = 0
        self.q_rr = 0
        self.self_known = {"V": 0, "S": 0}

    def emit(self, stream, fn, waits=(), inc=None, amt=1, forced=(), is_dma=False):
        k = self.know[stream]
        pruned = []

        def merge(sem, val):
            for s2, v2 in self.clock.get((sem, val), {sem: val}).items():
                if k.get(s2, 0) < v2:
                    k[s2] = v2

        if stream in ("V", "S") and fn is not None and not is_dma:
            prev = self.cnt[stream]
            if prev:
                pruned.append((stream, prev))
                merge(stream, prev)
                self.self_known[stream] = prev
            inc, amt = stream, 1
        for w in forced:
            if not w:
                continue
            pruned.append(w)
            merge(*w)
        for w in waits:
            if not w:
                continue
            sem, val = w
            if is_dma and sem == stream and stream in ("V", "S"):
                if val <= self.self_known[stream]:
                    continue
                pruned.append((sem, val))
                self.self_known[stream] = val
                merge(sem, val)
                continue
            if k.get(sem, 0) >= val:
                continue
            pruned.append((sem, val))
            merge(sem, val)
        tok = None
        if inc:
            self.cnt[inc] += amt
            tok = (inc, self.cnt[inc])
            snap = dict(k)
            snap[inc] = self.cnt[inc]
            self.clock[tok] = snap
            if inc == stream and stream in ("T", "V", "S"):
                k[inc] = self.cnt[inc]
        self.streams[stream].append((fn, pruned, tok, amt))
        return tok

    def alloc_sem(self, fam):
        if fam == "G":
            sem = f"G{self.g_rr % NG}"
            self.g_rr += 1
        else:
            sem = f"Q{self.q_rr % NQ}"
            self.q_rr += 1
        return sem

    def dma(self, stream, fn, waits=(), sem=None):
        if sem is None:
            sem = self.alloc_sem("G" if stream == "G" else "Q")
        prev = [(sem, self.cnt[sem])] if self.cnt[sem] else []
        return self.emit(stream, fn, waits, inc=sem, amt=16, forced=prev, is_dma=True)

    def replay(self, stream, eng, semh):
        for fn, waits, tok, amt in self.streams[stream]:
            for sem, val in waits:
                eng.wait_ge(semh[sem], val)
            inst = fn(eng) if fn is not None else None
            if tok is not None:
                inst.then_inc(semh[tok[0]], amt)


class Bank:
    __slots__ = ("w", "r")

    def __init__(self):
        self.w = None
        self.r = None


N_UNITS = {"wug_in": DK, "wug_g": DK, "wb": IK, "wc": IK, "wd": IK,
           "wout": DK, "wfg": 12, "wfu": 12, "wfd": DK}
UNIT_SZ = {r: 2048 for r in N_UNITS}
UNIT_SZ["wfd"] = HK * P
W_DT = {"wfg": I32, "wfu": I32, "wfd": I32}
SLOTS = {"wug_in": [0, 2048],
         "wout": [0, 2048, 4096, 6144],
         "wug_g": [4096, 6144],
         "wfg": [4096, 6144],
         "wb": [8192, 10240],
         "wc": [12288, 14336],
         "wfu": [20480, 22528],
         "wd": [16384, 18432],
         "wfd": [16384, 8192]}


def build_nc(n_chunks=8):
    toks = n_chunks * L
    nc = bass.Bass("TRN2", target_bir_lowering=False)
    d_x = nc.dram_tensor("x", [DM, toks], F32, kind="ExternalInput").ap()
    d_y = nc.dram_tensor("y", [DM, toks], F32, kind="ExternalOutput").ap()
    d_bias = nc.dram_tensor("bias", [P, 96], F32, kind="ExternalInput").ap()
    d_nw = nc.dram_tensor("nw", [P, DK], F32, kind="ExternalInput").ap()
    d_ob = nc.dram_tensor("ob", [P, DK], F32, kind="ExternalInput").ap()

    cw = {r: nc.dram_tensor(f"cw_{r}", [n, P, UNIT_SZ[r]], BF16, kind="Internal").ap()
          for r, n in N_UNITS.items()}
    dw = {r: nc.dram_tensor(f"dw_{r}", [n, P, UNIT_SZ[r]], W_DT.get(r, F32),
                            kind="ExternalInput").ap()
          for r, n in N_UNITS.items()}
    FAM_Q = {"wb": "Q", "wc": "Q", "wd": "Q", "wug_in": "Q", "wug_g": "Q",
             "wout": "G", "wfg": "G", "wfu": "G", "wfd": "G"}

    with contextlib.ExitStack() as ctx:
        e = ctx.enter_context
        s_bias = e(nc.sbuf_tensor("s_bias", [P, 96], F32))
        s_a = e(nc.sbuf_tensor("s_a", [P, IK], F32))
        s_h = e(nc.sbuf_tensor("s_h", [P, IK], F32))
        s_nw = e(nc.sbuf_tensor("s_nw", [P, DK], F32))
        s_ob = e(nc.sbuf_tensor("s_ob", [P, DK], F32))
        s_ones = e(nc.sbuf_tensor("s_ones", [P, P], BF16))
        s_x = e(nc.sbuf_tensor("s_x", [P, DK, L], F32))
        s_xbf = e(nc.sbuf_tensor("s_xbf", [P, DK, L], BF16))
        s_w = e(nc.sbuf_tensor("s_w", [P, DK, L], F32))
        s_x1 = e(nc.sbuf_tensor("s_x1", [P, DK, L], F32))
        s_x2 = e(nc.sbuf_tensor("s_x2", [P, DK, L], F32))
        s_xn = e(nc.sbuf_tensor("s_xn", [P, DK, L], BF16))
        s_xn2 = e(nc.sbuf_tensor("s_xn2", [P, DK, L], BF16))
        s_stt = e(nc.sbuf_tensor("s_stt", [P, 7, L], F32))
        s_sr = e(nc.sbuf_tensor("s_sr", [P, 2, L], F32))
        s_r2b = e(nc.sbuf_tensor("s_r2b", [P, L], BF16))
        s_uy = e(nc.sbuf_tensor("s_uy", [P, 2 * IK * L], BF16))
        s_bsb = e(nc.sbuf_tensor("s_bsb", [P, 2, L], F32))
        s_hs = e(nc.sbuf_tensor("s_hs", [P, 2, L], F32))
        s_csb = e(nc.sbuf_tensor("s_csb", [P, 2, L], F32))
        s_yv = e(nc.sbuf_tensor("s_yv", [P, 2, L], F32))
        s_gsb = e(nc.sbuf_tensor("s_gsb", [P, 2, L], F32))
        s_wa = e(nc.sbuf_tensor("s_wa", [P, 24576], BF16))
        pbank = [e(nc.psum_tensor(f"pb{i}", [P, 512], F32)) for i in range(8)]
        semh = {s: e(nc.semaphore(s)) for s in SEM_NAMES}

        u_v = s_uy[:, 0:IK * L].rearrange("p (k t) -> p k t", k=IK)
        y_v = s_uy[:, IK * L:2 * IK * L].rearrange("p (k t) -> p k t", k=IK)
        hf_v = s_uy[:, 0:HK * L].rearrange("p (k t) -> p k t", k=HK)
        ua0 = s_uy[:, 0:DK * L].rearrange("p (k t) -> p k t", k=DK)
        ua1 = s_uy[:, DK * L:2 * DK * L].rearrange("p (k t) -> p k t", k=DK)
        M1, R1, RMS1, M2, R2, T0c, Vc = range(7)

        em = Em()
        banks = [Bank() for _ in range(8)]

        occ = {}
        pending_wb = {}

        def slot_of(role, seq):
            sl = SLOTS[role]
            return (sl[seq % len(sl)], UNIT_SZ[role])

        def slot_flat(slot):
            return s_wa[:, slot[0]:slot[0] + slot[1]]

        def wview(role, seq, kcnt):
            return slot_flat(slot_of(role, seq)).rearrange("p (k j) -> p k j", k=kcnt)

        def slot_used(role, seq, tok):
            occ.setdefault(slot_of(role, seq), {}).setdefault(role, []).append(tok)

        def overlaps(a, b):
            return a[0] < b[0] + b[1] and b[0] < a[0] + a[1]

        def load_w(role, seq, unit, ch, sem=None):
            slot = slot_of(role, seq)
            ws = []
            for oslot in list(occ):
                if not overlaps(oslot, slot):
                    continue
                o = occ[oslot]
                for r2 in list(o):
                    key2 = (r2, oslot)
                    if key2 in pending_wb:
                        wunit, ltok = pending_wb.pop(key2)
                        wtok = em.dma("Q", (lambda r2=r2, wunit=wunit, sf=slot_flat(oslot):
                                            lambda g: g.dma_start(cw[r2][wunit], sf))(),
                                      waits=[ltok])
                        o.setdefault(r2, []).append(wtok)
                for lst in o.values():
                    ws.extend(lst)
                del occ[oslot]
            occ[slot] = {role: []}
            if ch == 0:
                tok = em.dma("G", (lambda dst=slot_flat(slot), role=role, unit=unit:
                                   lambda g: g.dma_start(dst, dw[role][unit]))(),
                             waits=ws, sem=sem)
                pending_wb[(role, slot)] = (unit, tok)
            else:
                tok = em.dma(FAM_Q[role], (lambda dst=slot_flat(slot), role=role,
                                           unit=unit: lambda g: g.dma_start(dst, cw[role][unit]))(),
                             waits=ws, sem=sem)
            return tok

        def flush_wb(roles):
            for key2 in list(pending_wb):
                r2, slot = key2
                if r2 not in roles:
                    continue
                wunit, ltok = pending_wb.pop(key2)
                wtok = em.dma("Q", (lambda r2=r2, wunit=wunit, sf=slot_flat(slot):
                                    lambda g: g.dma_start(cw[r2][wunit], sf))(),
                              waits=[ltok])
                occ.setdefault(slot, {}).setdefault(r2, []).append(wtok)

        tok_bias = em.dma("Q", lambda g: g.dma_start(s_bias[:], d_bias))
        tok_nw = em.dma("Q", lambda g: g.dma_start(s_nw[:], d_nw))
        tok_ob = em.dma("Q", lambda g: g.dma_start(s_ob[:], d_ob))
        tok_ones = em.emit("V", lambda v: v.memset(s_ones[:], 1.0), inc="V")
        tok_a = em.emit("S", lambda s: s.activation(
            s_a[:], s_bias[:, 80:96], AF.Sigmoid, scale=-1.0),
            waits=[tok_bias], inc="S")
        em.emit("V", lambda v: v.memset(s_h[:], 0.0))

        x_war = [() for _ in range(DK)]
        xbf_war = [None]
        xn_war = [None]
        xn2_war = [None]
        x2_war = [None] * DK
        scr_war = [None]
        r_war = [None, None]

        def mm(bank_i, lhsT, rhs, start, stop, waits=(), inc=False):
            b = banks[bank_i]
            ws = list(waits)
            if start:
                ws.append(b.r)
            tok = em.emit(
                "T",
                (lambda bi=bank_i, lh=lhsT, rh=rhs, st=start, sp=stop:
                 lambda t: t.matmul(pbank[bi][:], lh, rh, start=st, stop=sp))(),
                waits=ws, inc="T" if inc else None)
            if stop and inc:
                b.w = tok
            return tok

        def ln_chain(b_sum, b_sq, mc, rc_col, rms_col, ln, rc_out=None):
            m_ = s_stt[:, mc]
            t0 = s_stt[:, T0c]
            v_ = s_stt[:, Vc]
            tok_m = em.emit("V", (lambda m_=m_, b=b_sum: lambda v: v.tensor_scalar(
                m_, pbank[b][:], 1.0 / DM, None, ALU.mult))(),
                waits=[banks[b_sum].w, scr_war[0]], inc="V")
            banks[b_sum].r = tok_m
            tok_t0 = em.emit("V", (lambda t0=t0, b=b_sq: lambda v: v.tensor_scalar(
                t0, pbank[b][:], 1.0 / DM, None, ALU.mult))(),
                waits=[banks[b_sq].w], inc="V")
            banks[b_sq].r = tok_t0
            em.emit("V", (lambda v_=v_, m_=m_: lambda v: v.scalar_tensor_tensor(
                v_, m_, -1.0, m_, ALU.mult, ALU.mult))())
            tok_vp = em.emit("V", (lambda v_=v_, t0=t0: lambda v: v.scalar_tensor_tensor(
                v_, t0, EPS_LN, v_, ALU.add, ALU.add))(), inc="V")
            em.emit("S", (lambda t0=t0, v_=v_: lambda s: s.activation(
                t0, v_, AF.Ln))(), waits=[tok_vp])
            rdst = s_stt[:, rc_col] if rc_out is None else rc_out
            tok_r = em.emit("S", (lambda rdst=rdst, t0=t0: lambda s: s.activation(
                rdst, t0, AF.Exp, scale=-0.5))(),
                waits=[r_war[ln]], inc="S")
            tok_rms = None
            if rms_col is not None:
                tok_rc = em.emit("S", (lambda v_=v_, t0=t0: lambda s: s.activation(
                    v_, t0, AF.Exp, scale=-1.0))(), inc="S")
                tok_q = em.emit("V", (lambda v_=v_: lambda v: v.tensor_scalar(
                    v_, v_, -EPS_LN, 1.0 + EPS_RMS, ALU.mult, ALU.add))(),
                    waits=[tok_rc], inc="V")
                em.emit("S", (lambda t0=t0, v_=v_: lambda s: s.activation(
                    t0, v_, AF.Ln))(), waits=[tok_q])
                tok_rms = em.emit("S", (lambda rms_col=rms_col, t0=t0: lambda s: s.activation(
                    s_stt[:, rms_col], t0, AF.Exp, scale=-0.5))(), inc="S")
            scr_war[0] = tok_rms if tok_rms is not None else tok_r
            return tok_m, tok_r, tok_rms

        def emit_x_copies(c):
            t0_ = c * L
            st = {"c": c, "tok_x": [], "tok_cp": [], "tok_xn": [],
                  "tok_summ": [], "tok_sq": [None] * DK}
            for f in range(DK):
                st["tok_x"].append(em.dma("Q", (lambda f=f, t0_=t0_: lambda g: g.dma_start(
                    s_x[:, f], d_x[P * f:P * (f + 1), t0_:t0_ + L]))(),
                    waits=list(x_war[f])))
            for f in range(DK):
                st["tok_cp"].append(em.emit("V", (lambda f=f: lambda v: v.tensor_copy(
                    s_xbf[:, f], s_x[:, f]))(),
                    waits=[st["tok_x"][f], xbf_war[0]], inc="V"))
            return st

        def emit_stats1_mms(st):
            for f in range(DK):
                st["tok_summ"].append(mm(6, s_ones[:], s_xbf[:, f], f == 0, f == DK - 1,
                                         waits=[st["tok_cp"][f], tok_ones], inc=True))

        def emit_sq1(st, f):
            st["tok_sq"][f] = em.emit("V", (lambda f=f: lambda v: v.tensor_mul(
                s_xbf[:, f], s_xbf[:, f], s_xbf[:, f]))(),
                waits=[st["tok_summ"][f]], inc="V")

        def emit_stats1_tail(st):
            tok_sqm = None
            for f in range(DK):
                tok_sqm = mm(7, s_ones[:], s_xbf[:, f], f == 0, f == DK - 1,
                             waits=[st["tok_sq"][f]], inc=True)
            xbf_war[0] = tok_sqm
            st["ln"] = ln_chain(6, 7, M1, R1, RMS1, 0)

        def fb_ops(st, f):
            tok_m, tok_r, tok_rms = st["ln"]
            sl = f % 2
            em.emit("V", (lambda f=f, sl=sl: lambda v: v.tensor_sub(
                s_sr[:, sl], s_x[:, f], s_stt[:, M1]))(), waits=[tok_m])
            em.emit("V", (lambda sl=sl: lambda v: v.tensor_mul(
                s_sr[:, sl], s_sr[:, sl], s_stt[:, R1]))(), waits=[tok_r])
            tok = em.emit("V", (lambda f=f, sl=sl: lambda v: v.scalar_tensor_tensor(
                s_xn[:, f], s_sr[:, sl], s_nw[:, f:f + 1], s_stt[:, RMS1],
                ALU.mult, ALU.mult))(),
                waits=[tok_rms, tok_nw, xn_war[0]], inc="V")
            st["tok_xn"].append(tok)
            if f == DK - 1:
                r_war[0] = tok

        def fb_w(st, f):
            tok_w = em.emit("V", (lambda f=f, sl=f % 2: lambda v: v.tensor_add(
                s_w[:, f], s_x[:, f], s_sr[:, sl]))(), inc="V")
            x_war[f] = (tok_w,)

        FB_SCHED = [["f0", "f1", "w0"], ["f2", "w1", "f3", "w2"],
                    ["f4", "w3", "f5", "w4"], ["f6", "w5", "f7", "w6", "w7"]]

        st_cur = emit_x_copies(0)
        st_cur["wug_i"], st_cur["wug_g"] = {}, {}
        for gg in range(2):
            st_cur["wug_i"][gg] = load_w("wug_in", gg, gg, 0)
            st_cur["wug_g"][gg] = load_w("wug_g", gg, gg, 0)
        emit_stats1_mms(st_cur)
        for f in range(DK):
            emit_sq1(st_cur, f)
        emit_stats1_tail(st_cur)
        for f in range(DK):
            fb_ops(st_cur, f)
            fb_w(st_cur, f)
        st_cur["wb"], st_cur["wc"], st_cur["wd"] = {}, {}, {}
        for j in range(2):
            st_cur["wb"][j] = load_w("wb", j, j, 0)
            st_cur["wc"][j] = load_w("wc", j, j, 0)
            st_cur["wd"][j] = load_w("wd", j, j, 0)

        wug_in = s_wa[:, 0:4096].rearrange("p (b k j) -> p b k j", b=2, k=DK)
        wug_g = s_wa[:, 4096:8192].rearrange("p (b k j) -> p b k j", b=2, k=DK)
        wfg_v = s_wa[:, 4096:8192].rearrange("p (b k j) -> p b k j", b=2, k=DK)
        wfu_v = s_wa[:, 20480:24576].rearrange("p (b k j) -> p b k j", b=2, k=DK)
        for c in range(n_chunks):
            t0_ = c * L

            tok_wug_i, tok_wug_g = st_cur["wug_i"], st_cur["wug_g"]
            tok_xn = st_cur["tok_xn"]
            tok_u = [None] * IK
            gsb_war = [None, None]
            for jg in range(IK):
                gg, off = jg // 2, 128 * (jg % 2)
                bu, bg = jg % 2, 2 + jg % 2
                for k in range(DK):
                    mm(bu, wug_in[:, gg % 2, k, off:off + 128], s_xn[:, k, :],
                       k == 0, k == DK - 1,
                       waits=([tok_wug_i[gg], tok_xn[k]] if jg == 0 else
                              ([tok_wug_i[gg]] if k == 0 else [])),
                       inc=(k == DK - 1))
                slot_used("wug_in", gg, banks[bu].w)
                for k in range(DK):
                    mm(bg, wug_g[:, gg % 2, k, off:off + 128], s_xn[:, k, :],
                       k == 0, k == DK - 1,
                       waits=[tok_wug_g[gg]] if k == 0 else [],
                       inc=(k == DK - 1))
                slot_used("wug_g", gg, banks[bg].w)
                if jg % 2 == 1 and gg + 2 < DK:
                    g2 = gg + 2
                    tok_wug_i[g2] = load_w("wug_in", g2, g2, c)
                    tok_wug_g[g2] = load_w("wug_g", g2, g2, c)
                tok_sig = em.emit("S", (lambda jg=jg, bg=bg: lambda s: s.activation(
                    s_gsb[:, jg % 2], pbank[bg][:], AF.Sigmoid,
                    bias=s_bias[:, 16 + jg:17 + jg]))(),
                    waits=[banks[bg].w, gsb_war[jg % 2]], inc="S")
                tok_u[jg] = em.emit("V", (lambda jg=jg, bu=bu: lambda v: v.scalar_tensor_tensor(
                    u_v[:, jg, :], pbank[bu][:], s_bias[:, jg:jg + 1], s_gsb[:, jg % 2],
                    ALU.add, ALU.mult))(),
                    waits=[banks[bu].w, tok_sig], inc="V")
                banks[bu].r = tok_u[jg]
                banks[bg].r = tok_u[jg]
                gsb_war[jg % 2] = tok_u[jg]

            st_next = emit_x_copies(c + 1) if c + 1 < n_chunks else None

            tok_wb, tok_wc, tok_wd = st_cur["wb"], st_cur["wc"], st_cur["wd"]
            tok_scan = [None] * IK
            tok_yv = [None] * IK
            tok_y = [None] * IK
            tok_wout = {}
            for j in range(IK):
                bb_, bc_, bd_ = j % 2, 2 + j % 2, 4 + j % 2
                for k in range(IK):
                    mm(bb_, wview("wb", j, IK)[:, k, :], u_v[:, k, :], k == 0, k == IK - 1,
                       waits=[tok_wb[j], tok_u[-1]] if k == 0 else [], inc=(k == IK - 1))
                slot_used("wb", j, banks[bb_].w)
                tok_bsb = em.emit("S", (lambda j=j, bb_=bb_: lambda s: s.activation(
                    s_bsb[:, j % 2], pbank[bb_][:], AF.Identity,
                    bias=s_bias[:, 32 + j:33 + j]))(),
                    waits=[banks[bb_].w, tok_scan[j - 2] if j >= 2 else None], inc="S")
                tok_scan[j] = em.emit("V", (lambda j=j: lambda v: v.tensor_tensor_scan(
                    s_hs[:, j % 2], s_a[:, j:j + 1].broadcast_to((P, L)),
                    s_bsb[:, j % 2], s_h[:, j:j + 1], ALU.mult, ALU.add))(),
                    waits=[tok_bsb, tok_a], inc="V")
                em.emit("V", (lambda j=j: lambda v: v.tensor_copy(
                    s_h[:, j:j + 1], s_hs[:, j % 2, L - 1:L]))())
                for k in range(IK):
                    mm(bc_, wview("wc", j, IK)[:, k, :], u_v[:, k, :], k == 0, k == IK - 1,
                       waits=[tok_wc[j]] if k == 0 else [], inc=(k == IK - 1))
                slot_used("wc", j, banks[bc_].w)
                tok_csb = em.emit("S", (lambda j=j, bc_=bc_: lambda s: s.activation(
                    s_csb[:, j % 2], pbank[bc_][:], AF.Identity,
                    bias=s_bias[:, 48 + j:49 + j]))(),
                    waits=[banks[bc_].w, tok_yv[j - 2] if j >= 2 else None], inc="S")
                for k in range(IK):
                    mm(bd_, wview("wd", j, IK)[:, k, :], u_v[:, k, :], k == 0, k == IK - 1,
                       waits=[tok_wd[j]] if k == 0 else [], inc=(k == IK - 1))
                slot_used("wd", j, banks[bd_].w)
                if j + 2 < IK:
                    j2 = j + 2
                    tok_wb[j2] = load_w("wb", j2, j2, c)
                    tok_wc[j2] = load_w("wc", j2, j2, c)
                    tok_wd[j2] = load_w("wd", j2, j2, c)
                elif j == IK - 2:
                    for sq_ in range(4):
                        tok_wout[sq_] = load_w("wout", sq_, sq_, c)
                tok_yv[j] = em.emit("V", (lambda j=j: lambda v: v.tensor_mul(
                    s_yv[:, j % 2], s_csb[:, j % 2], s_hs[:, j % 2]))(),
                    waits=[tok_csb], inc="V")
                tok_y[j] = em.emit("V", (lambda j=j, bd_=bd_: lambda v: v.scalar_tensor_tensor(
                    y_v[:, j, :], pbank[bd_][:], s_bias[:, 64 + j:65 + j], s_yv[:, j % 2],
                    ALU.add, ALU.add))(),
                    waits=[banks[bd_].w], inc="V")
                banks[bb_].r = tok_y[j]
                banks[bc_].r = tok_y[j]
                banks[bd_].r = tok_y[j]

            tok_x1 = [None] * DK
            tok_cp2 = [None] * DK
            tok_sq2 = [None] * DK
            for j in range(DK):
                bo = j % 2
                for k in range(IK):
                    ws = []
                    if k == 0:
                        ws = [tok_wout[j]]
                    if j == 0:
                        ws = ws + [tok_y[k]]
                    mm(bo, wview("wout", j, IK)[:, k, :], y_v[:, k, :],
                       k == 0, k == IK - 1, waits=ws, inc=(k == IK - 1))
                slot_used("wout", j, banks[bo].w)
                if j + 4 < DK:
                    tok_wout[j + 4] = load_w("wout", j + 4, j + 4, c)
                tok_x1[j] = em.emit("V", (lambda j=j, bo=bo: lambda v: v.scalar_tensor_tensor(
                    s_x1[:, j], pbank[bo][:], s_ob[:, j:j + 1], s_w[:, j],
                    ALU.add, ALU.add))(),
                    waits=[banks[bo].w, tok_ob], inc="V")
                tok_cp2[j] = em.emit("S", (lambda j=j: lambda s: s.activation(
                    ua0[:, j], s_x1[:, j], AF.Identity))(),
                    waits=[tok_x1[j]], inc="S")
                banks[bo].r = tok_cp2[j]
                tok_sq2[j] = em.emit("S", (lambda j=j: lambda s: s.activation(
                    ua1[:, j], ua0[:, j], AF.Square))(), inc="S")
                if j >= 1:
                    mm(2, s_ones[:], ua0[:, j - 1], j - 1 == 0, j - 1 == DK - 1,
                       waits=[tok_cp2[j - 1]], inc=True)
                if j >= 2:
                    mm(3, s_ones[:], ua1[:, j - 2], j - 2 == 0, j - 2 == DK - 1,
                       waits=[tok_sq2[j - 2]], inc=True)
            mm(2, s_ones[:], ua0[:, DK - 1], False, True,
               waits=[tok_cp2[DK - 1]], inc=True)
            for j in (DK - 2, DK - 1):
                mm(3, s_ones[:], ua1[:, j], False, j == DK - 1,
                   waits=[tok_sq2[j]], inc=True)
            tok_m2, tok_r2, _ = ln_chain(2, 3, M2, R2, None, 1, rc_out=s_r2b[:])
            dtok = None
            for i in range(20):
                dtok = em.emit("T", (lambda i=i: lambda t: t.matmul(
                    pbank[6][:], s_ones[:], s_xn[:, i % DK, :],
                    start=True, stop=True))(),
                    waits=[banks[6].r] if i == 0 else [],
                    inc="T" if i == 19 else None)
            xn_war[0] = dtok
            if st_next is not None:
                emit_stats1_mms(st_next)
            tok_wfg, tok_wfu = {}, {}
            for gh in range(2):
                tok_wfg[gh] = load_w("wfg", gh, gh, c)
                tok_wfu[gh] = load_w("wfu", gh, gh, c)
            tok_wfd = {}
            for j in range(2):
                tok_wfd[j] = load_w("wfd", j, j, c)
            HF = DK // 2
            m2_b = s_stt[:, M2:M2 + 1].broadcast_to((P, HF, L))
            r2_b = s_r2b[:].rearrange("p (o t) -> p o t", o=1).broadcast_to((P, HF, L))
            tok_xn2h = [None, None]
            for hb in range(2):
                em.emit("V", (lambda hb=hb, m2_b=m2_b: lambda v: v.tensor_sub(
                    s_xn2[:, hb * HF:(hb + 1) * HF], s_x1[:, hb * HF:(hb + 1) * HF],
                    m2_b))(), waits=[tok_m2, xn2_war[0]] if hb == 0 else [])
                tok_xn2h[hb] = em.emit("V", (lambda hb=hb, r2_b=r2_b: lambda v: v.tensor_mul(
                    s_xn2[:, hb * HF:(hb + 1) * HF], s_xn2[:, hb * HF:(hb + 1) * HF],
                    r2_b))(), waits=[tok_r2] if hb == 0 else [], inc="V")
            r_war[1] = tok_xn2h[1]

            if c == 0:
                flush_wb({"wug_in", "wug_g", "wb", "wc", "wd", "wout"})
            tok_h = [None] * HK
            sg_war = [None, None]
            for jh in range(HK):
                gh, off = jh // 2, 128 * (jh % 2)
                bg_, bu_ = jh % 2, 2 + jh % 2
                for k in range(DK):
                    mm(bg_, wfg_v[:, gh % 2, k, off:off + 128], s_xn2[:, k, :],
                       k == 0, k == DK - 1,
                       waits=([tok_wfg[gh], tok_xn2h[0]] if jh == 0 and k == 0 else
                              ([tok_xn2h[1]] if jh == 0 and k == DK // 2 else
                               ([tok_wfg[gh]] if k == 0 else []))),
                       inc=(k == DK - 1))
                slot_used("wfg", gh, banks[bg_].w)
                for k in range(DK):
                    mm(bu_, wfu_v[:, gh % 2, k, off:off + 128], s_xn2[:, k, :],
                       k == 0, k == DK - 1,
                       waits=[tok_wfu[gh]] if k == 0 else [],
                       inc=(k == DK - 1))
                slot_used("wfu", gh, banks[bu_].w)
                if jh % 2 == 1 and gh + 2 < 12:
                    g2 = gh + 2
                    tok_wfg[g2] = load_w("wfg", g2, g2, c)
                    tok_wfu[g2] = load_w("wfu", g2, g2, c)
                tok_silu = em.emit("S", (lambda jh=jh, bg_=bg_: lambda s: s.activation(
                    s_gsb[:, jh % 2], pbank[bg_][:], AF.Silu, scale=FFS))(),
                    waits=[banks[bg_].w, sg_war[jh % 2]], inc="S")
                tok_h[jh] = em.emit("V", (lambda jh=jh, bu_=bu_: lambda v: v.scalar_tensor_tensor(
                    hf_v[:, jh, :], pbank[bu_][:], 1.0, s_gsb[:, jh % 2],
                    ALU.mult, ALU.mult))(),
                    waits=[banks[bu_].w, tok_silu], inc="V")
                banks[bg_].r = tok_h[jh]
                banks[bu_].r = tok_h[jh]
                sg_war[jh % 2] = tok_h[jh]
                if st_next is not None and 14 <= jh < 14 + DK:
                    emit_sq1(st_next, jh - 14)
            xn2_war[0] = banks[3].w

            if st_next is not None:
                st_next["wug_i"], st_next["wug_g"] = {}, {}
                for gg in range(2):
                    st_next["wug_i"][gg] = load_w("wug_in", gg, gg, c + 1)
                    st_next["wug_g"][gg] = load_w("wug_g", gg, gg, c + 1)

            tok_x2 = [None] * DK
            for j in range(DK):
                bo = 4 + j % 2
                for k in range(HK):
                    ws = []
                    if k == 0:
                        ws = [tok_wfd[j]]
                    if j == 0:
                        if k == 0:
                            ws = ws + [tok_h[20]]
                        elif k > 20:
                            ws = [tok_h[k]]
                    mm(bo, wview("wfd", j, HK)[:, k, :], hf_v[:, k, :],
                       k == 0, k == HK - 1, waits=ws, inc=(k == HK - 1))
                slot_used("wfd", j, banks[bo].w)
                if j + 2 < DK:
                    tok_wfd[j + 2] = load_w("wfd", j + 2, j + 2, c)
                tok_x2[j] = em.emit("V", (lambda j=j, bo=bo: lambda v: v.scalar_tensor_tensor(
                    s_x2[:, j], pbank[bo][:], FFS * FFS, s_x1[:, j],
                    ALU.mult, ALU.add))(),
                    waits=[banks[bo].w, x2_war[j]], inc="V")
                banks[bo].r = tok_x2[j]
                x2_war[j] = em.dma("Q", (lambda j=j, t0_=t0_: lambda g: g.dma_start(
                    d_y[P * j:P * (j + 1), t0_:t0_ + L], s_x2[:, j]))(),
                    waits=[tok_x2[j]])
                if st_next is not None:
                    if j == 0:
                        emit_stats1_tail(st_next)
                    elif j - 1 < len(FB_SCHED):
                        for task in FB_SCHED[j - 1]:
                            if task[0] == "f":
                                fb_ops(st_next, int(task[1]))
                            else:
                                fb_w(st_next, int(task[1]))

            if st_next is not None:
                st_next["wb"], st_next["wc"], st_next["wd"] = {}, {}, {}
                for j in range(2):
                    st_next["wb"][j] = load_w("wb", j, j, c + 1)
                    st_next["wc"][j] = load_w("wc", j, j, c + 1)
                    st_next["wd"][j] = load_w("wd", j, j, c + 1)
                st_cur = st_next

        em.emit("Q", None,
                forced=[(f"Q{i}", em.cnt[f"Q{i}"]) for i in range(NQ)
                        if em.cnt[f"Q{i}"]])

        with nc.Block() as block:
            @block.tensor
            def _(t):
                em.replay("T", t, semh)

            @block.vector
            def _(v):
                em.replay("V", v, semh)

            @block.scalar
            def _(s):
                em.replay("S", s, semh)

            @block.gpsimd
            def _(g):
                em.replay("G", g, semh)

            @block.sync
            def _(q):
                em.replay("Q", q, semh)

    return nc


def _prep_host(inputs):
    asc = np.ascontiguousarray
    f32 = np.float32

    def units_kj(wT, nk, junit):
        nu = wT.shape[1] // junit
        t = wT.reshape(nk, P, nu, junit)
        return asc(t.transpose(2, 1, 0, 3).reshape(nu, P, nk * junit))

    winT = np.asarray(inputs["L0_attn_in_w"], f32).T
    wgtT = np.asarray(inputs["L0_attn_gate_w"], f32).T
    wbT = np.asarray(inputs["L0_attn_b_w"], f32).T
    wcT = np.asarray(inputs["L0_attn_c_w"], f32).T
    wdT = np.asarray(inputs["L0_attn_d_w"], f32).T
    woutT = np.asarray(inputs["L0_attn_out_w"], f32).T
    wfgT = np.asarray(inputs["L0_ffn_w_gate"], np.int32).T
    wfuT = np.asarray(inputs["L0_ffn_w_up"], np.int32).T
    wfdT = np.asarray(inputs["L0_ffn_w_down"], np.int32).T
    out = dict(
        dw_wug_in=units_kj(winT, DK, 256),
        dw_wug_g=units_kj(wgtT, DK, 256),
        dw_wb=units_kj(wbT, IK, 128),
        dw_wc=units_kj(wcT, IK, 128),
        dw_wd=units_kj(wdT, IK, 128),
        dw_wout=units_kj(woutT, IK, 128),
        dw_wfg=units_kj(wfgT, DK, 256),
        dw_wfu=units_kj(wfuT, DK, 256),
        dw_wfd=units_kj(wfdT, HK, 128),
    )

    def col(v):
        return asc(np.asarray(v, f32).reshape(IK, P).T)

    out["bias"] = asc(np.concatenate([
        col(inputs["L0_attn_in_b"]), col(inputs["L0_attn_gate_b"]),
        col(inputs["L0_attn_b_b"]), col(inputs["L0_attn_c_b"]),
        col(inputs["L0_attn_d_b"]), col(inputs["L0_attn_a_log"])], axis=1))
    out["nw"] = asc(np.asarray(inputs["L0_attn_norm_w"], f32).reshape(DK, P).T)
    out["ob"] = asc(np.asarray(inputs["L0_attn_out_b"], f32).reshape(DK, P).T)
    return out


def kernel(**inputs):
    x = np.asarray(inputs["L0_attn_x"], np.float32)
    B = x.shape[0]
    shared = _prep_host(inputs)
    nc = build_nc(n_chunks=x.shape[1] // L)
    in_maps = []
    for b in range(B):
        m = dict(shared)
        m["x"] = np.ascontiguousarray(x[b].T)
        in_maps.append(m)
    res = run_bass_kernel_spmd(nc, in_maps, core_ids=list(range(B)))
    return np.stack([np.ascontiguousarray(r["y"].T) for r in res.results], axis=0)
